# version 21
# baseline (speedup 1.0000x reference)
import math
import numpy as np
import concourse.bass as bass
import concourse.mybir as mybir
from concourse.bass_utils import run_bass_kernel_spmd

F32 = mybir.dt.float32
I32 = mybir.dt.int32
BF16 = mybir.dt.bfloat16
AF = mybir.ActivationFunctionType
ALU = mybir.AluOpType
AX = mybir.AxisListType.X

B, S, D = 16, 2048, 1024
NCORE = 8
SEQ_PER_CORE = B // NCORE
TOK = SEQ_PER_CORE * S
T = 512
NT = TOK // T
TPS = S // T
NCH = D // 128
CTX = 4 * D
H1 = 2 * D
INNER = 4 * D
NCTX = CTX // 128
NH1 = H1 // 128
NIN = INNER // 128
NB = T // 128
LN_EPS = 1e-5

TWO_PI = 2.0 * math.pi
INV_2PI = float(np.float32(1.0 / TWO_PI))
CW1 = float(np.float32(TWO_PI))
CW2 = float(np.float32(TWO_PI - CW1))
PI_B = float(np.float32(np.nextafter(np.float32(math.pi), np.float32(0.0))))
PIH_HI = float(np.float32(math.pi / 2))
PIH_LO = float(np.float32(math.pi / 2 - PIH_HI))
LN127 = float(np.float32(math.log(127.0)))

_DBG_WANT = []
_TRUNC = None


class Emitter:
    ENGINES = ("te", "ve", "ac", "sy", "gp")
    NDS = 8

    def __init__(self):
        self.steps = []

    def emit(self, eng, fn, reads=(), writes=(), dma=False, async_p=False,
             no_inc=False):
        self.steps.append((eng, fn, tuple(reads), tuple(writes), dma, async_p,
                           no_inc))

    def dma(self, queue, fn, reads=(), writes=()):
        self.emit(queue, fn, reads, writes, dma=True)

    def build(self, block, sems, dma_sems):
        if _TRUNC is not None:
            self.steps = self.steps[:_TRUNC]
        steps = self.steps
        n = len(steps)
        NDS = self.NDS
        last_write = {}
        readers = {}
        deps = [set() for _ in range(n)]
        pre_dcnt = {"sy": 0, "ac": 0, "gp": 0}
        war_key = [None] * n
        for i, (eng, fn, reads, writes, is_dma, _, _) in enumerate(steps):
            if is_dma:
                war_key[i] = (eng, "dma", pre_dcnt[eng] % NDS)
                pre_dcnt[eng] += 1
            else:
                war_key[i] = (eng,)
            for r in reads:
                if r in last_write:
                    deps[i].add(last_write[r])
            for w in writes:
                if w in last_write:
                    deps[i].add(last_write[w])
                by_stream = {}
                for rd in readers.get(w, ()):
                    by_stream[war_key[rd]] = rd
                deps[i].update(by_stream.values())
            for r in reads:
                readers.setdefault(r, []).append(i)
            for w in writes:
                last_write[w] = i
                readers[w] = []
        flagged = [False] * n
        for i, dset in enumerate(deps):
            for j in dset:
                if steps[j][4]:
                    continue
                if steps[j][6]:
                    continue
                if (steps[j][0] != steps[i][0] or steps[j][5]
                        or steps[j][0] in ("ac", "ve", "gp")):
                    flagged[j] = True
        NDS = self.NDS
        seqno = [0] * n
        dslot = [0] * n
        cnt = {e: 0 for e in self.ENGINES}
        dcnt = {"sy": 0, "ac": 0, "gp": 0}
        slot_cnt = {}
        for i, st in enumerate(steps):
            if st[4]:
                s = dcnt[st[0]] % NDS
                dcnt[st[0]] += 1
                dslot[i] = s
                slot_cnt[(st[0], s)] = slot_cnt.get((st[0], s), 0) + 1
                seqno[i] = slot_cnt[(st[0], s)]
            elif flagged[i]:
                cnt[st[0]] += 1
                seqno[i] = cnt[st[0]]
        per_eng = {e: [] for e in self.ENGINES}
        for i, st in enumerate(steps):
            per_eng[st[0]].append(i)

        def run_engine(eng_name, eng):
            waited = {}

            def wait(key, sem, val):
                if waited.get(key, 0) >= val:
                    return
                eng.wait_ge(sem, val)
                waited[key] = val

            for i in per_eng[eng_name]:
                _, fn, reads, writes, is_dma, _asy, _ni = steps[i]
                need = {}
                for j in deps[i]:
                    src = steps[j][0]
                    if steps[j][4]:
                        key = (src, "dma", dslot[j])
                        val = seqno[j] * 16
                        sem = dma_sems[src][dslot[j]]
                    else:
                        if src == eng_name and not flagged[j]:
                            continue
                        key, val, sem = (src, "sem"), seqno[j], sems[src]
                    if val > need.get(key, (0, None))[0]:
                        need[key] = (val, sem)
                for key, (val, sem) in sorted(need.items(), key=str):
                    wait(key, sem, val)
                if is_dma:
                    s = dslot[i]
                    if seqno[i] > 1:
                        wait((eng_name, "dma", s), dma_sems[eng_name][s],
                             (seqno[i] - 1) * 16)
                inst = fn(eng)
                if is_dma:
                    inst.then_inc(dma_sems[eng_name][dslot[i]], 16)
                elif flagged[i]:
                    inst.then_inc(sems[eng_name], 1)
            if eng_name == "sy":
                for q in ("sy", "ac", "gp"):
                    for s in range(NDS):
                        tot = slot_cnt.get((q, s), 0)
                        if tot > 0:
                            eng.wait_ge(dma_sems[q][s], tot * 16)

        @block.tensor
        def _(tensor):
            run_engine("te", tensor)

        @block.vector
        def _(vector):
            run_engine("ve", vector)

        @block.scalar
        def _(scalar):
            run_engine("ac", scalar)

        @block.sync
        def _(sync):
            run_engine("sy", sync)

        @block.gpsimd
        def _(gpsimd):
            run_engine("gp", gpsimd)


def build_program():
    nc = bass.Bass("TRN2", target_bir_lowering=False,
                   detect_race_conditions=False)

    x_h = nc.dram_tensor("x", [TOK, D], F32, kind="ExternalInput").ap()
    wOmT_h = nc.dram_tensor("wOmT", [D, D], F32, kind="ExternalInput").ap()
    w1T_h = nc.dram_tensor("w1T", [CTX, H1], F32, kind="ExternalInput").ap()
    w2T_h = nc.dram_tensor("w2T", [H1, D], F32, kind="ExternalInput").ap()
    wgT_h = nc.dram_tensor("wgT", [D, INNER], F32, kind="ExternalInput").ap()
    wuT_h = nc.dram_tensor("wuT", [D, INNER], F32, kind="ExternalInput").ap()
    wdT_h = nc.dram_tensor("wdT", [INNER, D], F32, kind="ExternalInput").ap()
    bom_h = nc.dram_tensor("bom", [128, NCH], F32, kind="ExternalInput").ap()
    lsc_h = nc.dram_tensor("lsc", [128, NCH], F32, kind="ExternalInput").ap()
    lng_h = nc.dram_tensor("lng", [128, NCTX], F32, kind="ExternalInput").ap()
    lnb_h = nc.dram_tensor("lnb", [128, NCTX], F32, kind="ExternalInput").ap()
    b1_h = nc.dram_tensor("b1v", [128, NH1], F32, kind="ExternalInput").ap()
    b2_h = nc.dram_tensor("b2v", [128, NCH], F32, kind="ExternalInput").ap()
    ident_h = nc.dram_tensor("ident", [128, 128], F32, kind="ExternalInput").ap()
    ones32_h = nc.dram_tensor("ones32", [128, 128], F32, kind="ExternalInput").ap()
    out_h = nc.dram_tensor("out", [TOK, D], F32, kind="ExternalOutput").ap()

    sOm_h = nc.dram_tensor("sOm", [NCH, 128, D], BF16, kind="Internal").ap()
    sW1_h = nc.dram_tensor("sW1", [NH1, 128, CTX], BF16, kind="Internal").ap()
    sW2_h = nc.dram_tensor("sW2", [NCH, 128, H1], BF16, kind="Internal").ap()
    sWg_h = nc.dram_tensor("sWg", [NIN, 128, D], BF16, kind="Internal").ap()
    sWu_h = nc.dram_tensor("sWu", [NIN, 128, D], BF16, kind="Internal").ap()
    sWd_h = nc.dram_tensor("sWd", [NIN, 128, D], BF16, kind="Internal").ap()

    dbg_hs = {}
    for name in _DBG_WANT:
        dbg_hs[name] = nc.dram_tensor("dbg_" + name, [NT, 128, NCH * T], F32,
                                      kind="ExternalOutput").ap()

    from contextlib import ExitStack
    es = ExitStack()
    with es:
        def sbt(name, shape, dt=F32):
            return es.enter_context(nc.sbuf_tensor(name, shape, dt))

        s1 = sbt("s1", [128, NCH * T]); s2 = sbt("s2", [128, NCH * T])
        s3 = sbt("s3", [128, NCH * T]); s4 = sbt("s4", [128, NCH * T])
        s5 = sbt("s5", [128, NCH * T])
        x_tm = sbt("x_tm", [128, 2 * D])
        hnb = sbt("hnb", [128, NCTX * T], BF16)
        glb = sbt("glb", [128, NH1 * T], BF16)
        wb = [sbt(f"wb{i}", [128, 2048], BF16) for i in range(3)]
        wsm = [sbt(f"wsm{i}", [128, 1024], BF16) for i in range(5)]
        csA = sbt("csA", [128, T]); csB = sbt("csB", [128, T])

        csC = sbt("csC", [128, T]); csD = sbt("csD", [128, T])
        sinc = sbt("sinc", [128, T]); cosc = sbt("cosc", [128, T])
        qi32 = sbt("qi32", [128, T], I32)
        sbA = sbt("sbA", [128, T], BF16); sbB = sbt("sbB", [128, T], BF16)
        mu_b = sbt("mu_b", [128, T]); rstd_b = sbt("rstd_b", [128, T])
        gps0 = mu_b; gps1 = rstd_b
        nmr_b = sbt("nmr_b", [128, T]); aux_b = sbt("aux_b", [128, T])
        ups0 = nmr_b; ups1 = aux_b
        aux2_b = csD
        s1q_b = sbt("s1q_b", [128, T]); cg_b = sbt("cg_b", [128, T])
        s2q_b = sbt("s2q_b", [128, T]); cfin_b = sbt("cfin_b", [128, T])
        posc_b = sbt("posc_b", [128, T]); ivp_b = sbt("ivp_b", [128, T])
        prow = sbt("prow", [1, T])
        zrow = sbt("zrow", [128, T + 2 * NB * 128 + 4])
        tmm = sbt("tmm", [128, NB]); tms = sbt("tms", [128, NB])
        tmq = sbt("tmq", [128, NB])
        tml = sbt("tml", [128, NB])
        tmc = sbt("tmc", [128, 2 * NB])
        rowst = sbt("rowst", [2 * NB, 128])
        identt = sbt("identt", [128, 128]); onest = sbt("onest", [128, 128])
        onebf = sbt("onebf", [128, 128], BF16)
        bom_t = sbt("bom_t", [128, NCH]); lsc_t = sbt("lsc_t", [128, NCH])
        els_t = sbt("els_t", [128, NCH]); bels_t = sbt("bels_t", [128, NCH])
        lng_t = sbt("lng_t", [128, NCTX]); lnb_t = sbt("lnb_t", [128, NCTX])
        b1_t = sbt("b1_t", [128, NH1]); b2_t = sbt("b2_t", [128, NCH])
        mg_t = sbt("mg_t", [128, 1]); mu_t = sbt("mu_t", [128, 1])
        md_t = sbt("md_t", [128, 1])
        sg_t = sbt("sg_t", [128, 1]); su_t = sbt("su_t", [128, 1])
        sd_t = sbt("sd_t", [128, 1])
        mud_t = sbt("mud_t", [128, 1]); mg127_t = sbt("mg127_t", [128, 1])
        accp = sbt("accp", [128, 1]); accw = sbt("accw", [128, 64])
        thrp_t = sbt("thrp_t", [128, 1]); thrn_t = sbt("thrn_t", [128, 1])
        sbC = sbt("sbC", [128, T], BF16); sbD = sbt("sbD", [128, T], BF16)
        tmn1 = sbt("tmn1", [128, 1]); tmn2 = sbt("tmn2", [128, 1])
        cst = sbt("cst", [128, 3])
        phc = sbt("phc", [128, NCH]); mrc = sbt("mrc", [128, NCH])
        mic = sbt("mic", [128, NCH])

        pb = [es.enter_context(nc.psum_tensor(f"pb{i}", [128, T], F32))
              for i in range(8)]

        sems = {e: es.enter_context(nc.semaphore("sem_" + e))
                for e in Emitter.ENGINES}
        dma_sems = {q: [es.enter_context(nc.semaphore(f"dsem_{q}{i}"))
                        for i in range(Emitter.NDS)]
                    for q in ("sy", "ac", "gp")}
        block = es.enter_context(nc.Block())

        E = Emitter()

        def rr(h, c=0):
            return (id(h), c)

        def rrs(h, cs):
            return [(id(h), c) for c in cs]

        def ts(out, in0, sc1, sc2, op0, op1=None, rd=(), wr=()):
            if op1 is None:
                E.emit("ve", lambda e, o=out, i=in0, a=sc1, p0=op0:
                       e.tensor_scalar(o, i, a, None, p0), rd, wr)
            else:
                E.emit("ve", lambda e, o=out, i=in0, a=sc1, b=sc2, p0=op0, p1=op1:
                       e.tensor_scalar(o, i, a, b, p0, p1), rd, wr)

        def tt(out, in0, in1, op, rd=(), wr=()):
            E.emit("ve", lambda e, o=out, i=in0, j=in1, p=op:
                   e.tensor_tensor(o, i, j, p), rd, wr)

        def gtt(out, in0, in1, op, rd=(), wr=()):
            E.emit("gp", lambda e, o=out, i=in0, j=in1, p=op:
                   e.tensor_tensor(o, i, j, p), rd, wr)

        def stt(out, in0, sc, in1, op0, op1, rd=(), wr=()):
            E.emit("ve", lambda e, o=out, i=in0, s=sc, j=in1, p0=op0, p1=op1:
                   e.scalar_tensor_tensor(o, i, s, j, p0, p1), rd, wr)

        def act(out, in_, func, bias=0.0, scale=1.0, accum=None, rd=(), wr=()):
            E.emit("ac", lambda e, o=out, i=in_, f=func, b=bias, s=scale, a=accum:
                   e.activation(o, i, f, bias=b, scale=s, accum_out=a), rd, wr)

        def vcp(out, in_, rd=(), wr=()):
            E.emit("ve", lambda e, o=out, i=in_: e.tensor_copy(o, i), rd, wr)

        def mm(out, lhsT, rhs, start, stop, rd=(), wr=()):
            E.emit("te", lambda e, o=out, l=lhsT, r=rhs, a=start, b=stop:
                   e.matmul(o, l, r, start=a, stop=b), rd, wr)

        def tr(out, in_, rd=(), wr=()):
            E.emit("te", lambda e, o=out, i=in_:
                   e.transpose(o, i, identt[:, :]), rd, wr)

        def dma(q, out, in_, rd=(), wr=()):
            E.dma(q, lambda e, o=out, i=in_: e.dma_start(out=o, in_=i), rd, wr)

        def scan(out, data0, init, rd=(), wr=()):
            E.emit("ve", lambda e, o=out, d=data0, v=init:
                   e.tensor_tensor_scan(o, d, d, v, ALU.add, ALU.bypass),
                   rd, wr, no_inc=True)

        dma("sy", identt[:, :], ident_h, wr=[rr(identt)])
        dma("sy", onest[:, :], ones32_h, wr=[rr(onest)])
        act(onebf[:, :], onest[:, :], AF.Identity, rd=[rr(onest)], wr=[rr(onebf)])
        dma("sy", bom_t[:, :], bom_h, wr=[rr(bom_t)])
        dma("sy", lsc_t[:, :], lsc_h, wr=[rr(lsc_t)])
        dma("sy", lng_t[:, :], lng_h, wr=[rr(lng_t)])
        dma("sy", lnb_t[:, :], lnb_h, wr=[rr(lnb_t)])
        dma("sy", b1_t[:, :], b1_h, wr=[rr(b1_t)])
        dma("sy", b2_t[:, :], b2_h, wr=[rr(b2_t)])
        act(els_t[:, :], lsc_t[:, :], AF.Exp, rd=[rr(lsc_t)], wr=[rr(els_t)])
        tt(bels_t[:, :], bom_t[:, :], els_t[:, :], ALU.mult,
           rd=[rr(bom_t), rr(els_t)], wr=[rr(bels_t)])

        E.emit("ve", lambda e: e.memset(prow[0:1, :], 1.0), (), [rr(prow)], async_p=True)
        E.emit("ve", lambda e: e.memset(zrow[:, :], 0.0), (),
               [rr(zrow, c) for c in range(5)], async_p=True)
        E.emit("ve", lambda e: e.memset(cst[:, 0:1], LN_EPS), (), [rr(cst)], async_p=True)
        E.emit("ve", lambda e: e.memset(cst[:, 1:2], LN127), (), [rr(cst)], async_p=True)
        E.emit("ve", lambda e: e.memset(cst[:, 2:3], 0.5), (), [rr(cst)], async_p=True)

        def cast_gen(src_h, dst_h, n_m, n_k, tag_h):
            for m in range(n_m):
                for kh in range((n_k + 15) // 16):
                    k0, k1 = kh * 16, min(n_k, (kh + 1) * 16)
                    src = src_h[k0 * 128:k1 * 128, m * 128:(m + 1) * 128] \
                        .rearrange("(k p) c -> p k c", p=128)
                    dst = dst_h[m, :, k0 * 128:k1 * 128] \
                        .rearrange("p (k c) -> p k c", c=128)
                    dma("gp", dst, src, wr=[rr(tag_h, m)])
                    yield

        tern_slabs = [(s1q_b[:, :], rr(s1q_b)), (s2q_b[:, :], rr(s2q_b))]
        tern_outs = [(cg_b[:, :], rr(cg_b)), (cfin_b[:, :], rr(cfin_b))]
        ZMCOL = T + 2 * NB * 128

        def tern_gen(src_h, dst_h, n_k, n_m, m_t, s_t, tag_h, base, kmajor,
                     post=None, gp_every=0):
            nelem = float(n_k * 128 * n_m * 128)
            P = (n_m * 128) // 512
            pieces = [(k, p) for k in range(n_k) for p in range(P)]
            n = len(pieces)
            for i in range(n + 1):
                if i < n:
                    k, p = pieces[i]
                    sl, sres = tern_slabs[i % 2]
                    dma("sy", sl, src_h[k * 128:(k + 1) * 128,
                                        p * 512:(p + 1) * 512], wr=[sres])
                    yield
                if i >= 1:
                    k, p = pieces[i - 1]
                    sl, sres = tern_slabs[(i - 1) % 2]
                    E.emit("ve", lambda e, sl=sl, c=base + i - 1:
                           e.reduce_sum(accw[:, c:c + 1], sl, AX,
                                        apply_absolute_value=True),
                           [sres], [rr(accw)])
                yield
            E.emit("ve", lambda e: e.reduce_sum(accp[:, 0:1],
                                                accw[:, base:base + n], AX),
                   [rr(accw)], [rr(accp)])
            E.emit("te", lambda e: e.matmul(pb[7][0:1, 0:1], onest[:, 0:1],
                                            accp[:, 0:1], start=True, stop=True),
                   [rr(onest), rr(accp)], [rr(pb[7])])
            ts(tmn1[0:1, 0:1], pb[7][0:1, 0:1], 1.0 / nelem, 1e-5,
               ALU.mult, ALU.max, rd=[rr(pb[7])], wr=[rr(tmn1)])
            vcp(zrow[0:1, ZMCOL:ZMCOL + 1], tmn1[0:1, 0:1],
                rd=[rr(tmn1)], wr=[rr(zrow, 2)])
            E.emit("te", lambda e: e.matmul(
                pb[7][:, 0:1], onest[:, :], zrow[:, ZMCOL:ZMCOL + 1],
                start=True, stop=True),
                   [rr(onest), rr(zrow, 2)], [rr(pb[7])])
            vcp(m_t[:, :], pb[7][:, 0:1], rd=[rr(pb[7])], wr=[rr(m_t)])
            ts(thrp_t[:, :], m_t[:, :], 0.5, None, ALU.mult,
               rd=[rr(m_t)], wr=[rr(thrp_t)])
            ts(thrn_t[:, :], m_t[:, :], -0.5, None, ALU.mult,
               rd=[rr(m_t)], wr=[rr(thrn_t)])
            if post is not None:
                post()
            yield
            if not kmajor:
                pieces = [(k, p) for p in range(P) for k in range(n_k)]
            for i in range(n + 1):
                if i < n:
                    k, p = pieces[i]
                    sl, sres = tern_slabs[i % 2]
                    dma("sy", sl, src_h[k * 128:(k + 1) * 128,
                                        p * 512:(p + 1) * 512], wr=[sres])
                    yield
                if i >= 1:
                    k, p = pieces[i - 1]
                    sl, sres = tern_slabs[(i - 1) % 2]
                    obf = sbC if (i - 1) % 2 == 0 else sbD
                    ofres = rr(obf)
                    on_gp = gp_every > 0 and (i - 1) % gp_every == gp_every - 1
                    def _p2(e, sl=sl, obf=obf):
                        return e.tensor_scalar(obf[:, :], sl, thrp_t[:, :],
                                               -1.0, ALU.is_gt, ALU.add)
                    def _p2b(e, sl=sl):
                        return e.tensor_scalar(sl, sl, thrn_t[:, :], None,
                                               ALU.is_ge)
                    def _p2c(e, sl=sl, obf=obf):
                        return e.tensor_tensor(obf[:, :], obf[:, :], sl,
                                               ALU.add)
                    engn = "gp" if on_gp else "ve"
                    E.emit(engn, _p2, [sres, rr(thrp_t)], [ofres])
                    E.emit(engn, _p2b, [sres, rr(thrn_t)], [sres])
                    E.emit(engn, _p2c, [ofres, sres], [ofres])
                    if kmajor:
                        dma("sy", dst_h[k, :, p * 512:(p + 1) * 512],
                            obf[:, :], rd=[ofres], wr=[rr(tag_h, k)])
                    else:
                        m0 = p * 4
                        dma("sy",
                            dst_h[m0:m0 + 4, :, k * 128:(k + 1) * 128]
                            .rearrange("m p c -> p m c"),
                            obf[:, :].rearrange("p (m c) -> p m c", c=128),
                            rd=[ofres],
                            wr=[rr(tag_h, mm_) for mm_ in range(m0, m0 + 4)])
                yield

        def post_wg():
            ts(mg127_t[:, :], mg_t[:, :], 1.0 / 127.0, None, ALU.mult,
               rd=[rr(mg_t)], wr=[rr(mg127_t)])

        def post_wd():
            tt(mud_t[:, :], mu_t[:, :], md_t[:, :], ALU.mult,
               rd=[rr(mu_t), rr(md_t)], wr=[rr(mud_t)])
            ts(mud_t[:, :], mud_t[:, :], 1.0 / (127.0 * 127.0), None, ALU.mult,
               rd=[rr(mud_t)], wr=[rr(mud_t)])

        prep_done = set()

        def prep_chain():
            for name, g in (
                    ("wg", tern_gen(wgT_h, sWg_h, NCH, NIN, mg_t, sg_t,
                                    sWg_h, 0, False, post_wg)),
                    ("wu", tern_gen(wuT_h, sWu_h, NCH, NIN, mu_t, su_t,
                                    sWu_h, 0, False)),
                    ("wd", tern_gen(wdT_h, sWd_h, NIN, NCH, md_t, sd_t,
                                    sWd_h, 0, True, post_wd))):
                for _ in g:
                    yield
                prep_done.add(name)

        prep_iter = prep_chain()

        def drain(nsteps):
            for _ in range(nsteps):
                try:
                    next(prep_iter)
                except StopIteration:
                    return

        def ensure(name):
            while name not in prep_done:
                try:
                    next(prep_iter)
                except StopIteration:
                    prep_done.add(name)
                    return

        ws_state = {"big": 0, "small": 0}

        def load_big(src_h, m, k0, nk):
            idx = ws_state["big"]
            ws_state["big"] = (idx + 1) % 3
            buf = wb[idx]
            apb = buf[:, :nk * 128]
            res = rr(buf)
            dma("gp", apb.rearrange("p (k c) -> p k c", c=128),
                src_h[k0 * 128:(k0 + nk) * 128, m * 128:(m + 1) * 128]
                .rearrange("(k p) c -> p k c", p=128),
                rd=[], wr=[res])
            return apb, res

        def load_small_cast(src_h, m, nk):
            idx = ws_state["small"]
            ws_state["small"] = (idx + 1) % 5
            buf = wsm[idx]
            apb = buf[:, :nk * 128]
            res = rr(buf)
            dma("gp", apb.rearrange("p (k c) -> p k c", c=128),
                src_h[0:nk * 128, m * 128:(m + 1) * 128]
                .rearrange("(k p) c -> p k c", p=128),
                rd=[], wr=[res])
            return apb, res

        def load_small(scr_h, m, ncols):
            idx = ws_state["small"]
            ws_state["small"] = (idx + 1) % 5
            buf = wsm[idx]
            apb = buf[:, :ncols]
            res = rr(buf)
            dma("sy", apb, scr_h[m, :, :ncols], rd=[rr(scr_h, m)], wr=[res])
            return apb, res

        def ln_finalize(D_, banks=(4, 5)):
            b0, b1 = banks
            act(mu_b[:, :], pb[b0][:, :], AF.Identity, scale=1.0 / D_,
                rd=[rr(pb[b0])], wr=[rr(mu_b)])
            act(aux_b[:, :], pb[b1][:, :], AF.Identity, scale=1.0 / D_,
                rd=[rr(pb[b1])], wr=[rr(aux_b)])
            act(aux2_b[:, :], mu_b[:, :], AF.Square,
                rd=[rr(mu_b)], wr=[rr(aux2_b)])
            tt(aux_b[:, :], aux_b[:, :], aux2_b[:, :], ALU.subtract,
               rd=[rr(aux_b), rr(aux2_b)], wr=[rr(aux_b)])
            act(aux2_b[:, :], aux_b[:, :], AF.Ln, bias=cst[:, 0:1],
                rd=[rr(aux_b), rr(cst)], wr=[rr(aux2_b)])
            act(rstd_b[:, :], aux2_b[:, :], AF.Exp, scale=-0.5,
                rd=[rr(aux2_b)], wr=[rr(rstd_b)])
            tt(nmr_b[:, :], mu_b[:, :], rstd_b[:, :], ALU.mult,
               rd=[rr(mu_b), rr(rstd_b)], wr=[rr(nmr_b)])

        def ln_stat_step(i, nck, a, res, stage="glb", banks=(4, 5)):
            b0, b1 = banks
            buf, bres = (sbA, rr(sbA)) if i % 2 == 0 else (sbB, rr(sbB))
            if stage == "glb":
                c2 = 14 + (i % 2)
                buf2, bres2 = glb[:, c2 * T:(c2 + 1) * T], rr(glb, c2)
            else:
                c2 = i % 2
                buf2, bres2 = hnb[:, c2 * T:(c2 + 1) * T], rr(hnb, c2)
            act(buf[:, :], a, AF.Identity, rd=[res], wr=[bres])
            mm(pb[b0][:, :], onebf[:, :], buf[:, :], i == 0, i == nck - 1,
               rd=[rr(onebf), bres], wr=[rr(pb[b0])])
            act(buf2, a, AF.Square, rd=[res], wr=[bres2])
            mm(pb[b1][:, :], onebf[:, :], buf2, i == 0, i == nck - 1,
               rd=[rr(onebf), bres2], wr=[rr(pb[b1])])

        def round_to_bf16(v_ap, v_res, out_ap, out_res):
            act(qi32[:, :], v_ap, AF.Identity, bias=cst[:, 2:3],
                rd=[v_res, rr(cst)], wr=[rr(qi32)])
            stt(csC[:, :], qi32[:, :], -1.0, v_ap, ALU.mult, ALU.add,
                rd=[rr(qi32), v_res], wr=[rr(csC)])
            ts(csD[:, :], csC[:, :], 0.5, -1.0, ALU.is_gt, ALU.add,
               rd=[rr(csC)], wr=[rr(csD)])
            stt(csD[:, :], qi32[:, :], 1.0, csD[:, :], ALU.mult, ALU.add,
                rd=[rr(qi32), rr(csD)], wr=[rr(csD)])
            stt(out_ap, csC[:, :], -0.5, csD[:, :], ALU.is_ge, ALU.add,
                rd=[rr(csC), rr(csD)], wr=[out_res])

        def bcast_pair(n_used, out0, out0_res, out1, out1_res):
            E.emit("te", lambda e:
                   e.transpose(pb[7][0:2 * NB, 0:128], tmc[:, :], identt[:, :]),
                   [rr(tmc), rr(identt)], [rr(pb[7])])
            vcp(rowst[:, :], pb[7][0:2 * NB, 0:128],
                rd=[rr(pb[7])], wr=[rr(rowst)])
            dma("gp", zrow[0:1, T:T + 2 * NB * 128], rowst[:, :],
                rd=[rr(rowst)], wr=[rr(zrow, 1)])
            E.emit("te", lambda e:
                   e.matmul(pb[7][:, :], onest[:, :], zrow[:, T:2 * T],
                            start=True, stop=True),
                   [rr(onest), rr(zrow, 1)], [rr(pb[7])])
            vcp(out0, pb[7][:, :], rd=[rr(pb[7])], wr=[out0_res])
            if n_used > 1:
                E.emit("te", lambda e:
                       e.matmul(pb[7][:, :], onest[:, :], zrow[:, 2 * T:3 * T],
                                start=True, stop=True),
                       [rr(onest), rr(zrow, 1)], [rr(pb[7])])
                vcp(out1, pb[7][:, :], rd=[rr(pb[7])], wr=[out1_res])

        def rowmax_acc(i, a, res):
            sq, sqres = ((ivp_b, rr(ivp_b)) if i % 2 == 0
                         else (cfin_b, rr(cfin_b)))
            tt(sq[:, :], a, a, ALU.mult, rd=[res], wr=[sqres])
            if i == 0:
                vcp(posc_b[:, :], sq[:, :], rd=[sqres], wr=[rr(posc_b)])
            else:
                tt(posc_b[:, :], posc_b[:, :], sq[:, :], ALU.max,
                   rd=[rr(posc_b), sqres], wr=[rr(posc_b)])

        def rowmax_finish(out_tm_ap, out_tm_res):
            for b in range(NB):
                E.emit("te", lambda e, b=b:
                       e.transpose(pb[6][:, b * 128:(b + 1) * 128],
                                   posc_b[:, b * 128:(b + 1) * 128],
                                   identt[:, :]),
                       [rr(posc_b), rr(identt)], [rr(pb[6])])
            for b in range(NB):
                E.emit("ve", lambda e, b=b:
                       e.reduce_max(tmm[:, b:b + 1],
                                    pb[6][:, b * 128:(b + 1) * 128], AX),
                       [rr(pb[6])], [rr(tmm)])
            ts(out_tm_ap, tmm[:, :], 1e-10, None, ALU.max,
               rd=[rr(tmm)], wr=[out_tm_res])

        def emit_tile(t):
            seq_t = t % TPS
            first_in_seq = (seq_t == 0)
            posc_ap = posc_b[:, :]
            ivp_ap = ivp_b[:, :]
            E.emit("ve", lambda e, off=float(seq_t * T):
                   e.tensor_tensor_scan(zrow[0:1, 0:T], prow[0:1, :],
                                        prow[0:1, :], off, ALU.add, ALU.bypass),
                   [rr(prow)], [rr(zrow, 0)], no_inc=True)
            vcp(csD[0:1, 0:T], zrow[0:1, 0:T],
                rd=[rr(zrow, 0)], wr=[rr(csD), ("possh", t)])
            E.emit("te", lambda e:
                   e.matmul(pb[6][:, :], onest[:, :], zrow[:, 0:T],
                            start=True, stop=True),
                   [rr(onest), rr(zrow, 0), ("possh", t)], [rr(pb[6])])
            act(aux_b[:, :], pb[6][:, :], AF.Ln, rd=[rr(pb[6])], wr=[rr(aux_b)])
            act(posc_b[:, :], aux_b[:, :], AF.Exp, scale=-0.5,
                rd=[rr(aux_b)], wr=[rr(posc_b)])
            act(ivp_b[:, :], aux_b[:, :], AF.Exp, scale=-1.0,
                rd=[rr(aux_b)], wr=[rr(ivp_b)])
            drain(4)
            xbf = hnb
            xq = glb

            for hh in range(2):
                dma("sy", x_tm[:].rearrange("p (c d) -> p c d", d=D),
                    x_h[t * T + hh * 256:t * T + (hh + 1) * 256, :]
                    .rearrange("(c p) d -> p c d", p=128),
                    rd=rrs(x_tm, range(2)), wr=rrs(x_tm, range(2)))
                for f in range(NCH):
                    for tc in range(2):
                        E.emit("te", lambda e, f=f, tc=tc:
                               e.transpose(pb[6][:, tc * 128:(tc + 1) * 128],
                                           x_tm[:, tc * D + f * 128:tc * D + (f + 1) * 128],
                                           identt[:, :]),
                               rrs(x_tm, range(2)) + [rr(identt)],
                               [rr(pb[6])])
                    h0 = hh * 256
                    act(s1[:, f * T + h0:f * T + h0 + 256], pb[6][:, 0:256],
                        AF.Identity, rd=[rr(pb[6])], wr=[rr(s1, f)])
                    act(xbf[:, f * T + h0:f * T + h0 + 256],
                        s1[:, f * T + h0:f * T + h0 + 256],
                        AF.Identity, rd=[rr(s1, f)], wr=[rr(hnb, f)])
                    drain(1)

            for m in range(NCH):
                slab, sres = load_small_cast(wOmT_h, m, NCH)
                bank = m % 2
                for k in range(NCH):
                    mm(pb[bank][:, :], slab[:, k * 128:(k + 1) * 128],
                       xbf[:, k * T:(k + 1) * T], k == 0, k == NCH - 1,
                       rd=[sres, rr(hnb, k)], wr=[rr(pb[bank])])
                act(csA[:, :], pb[bank][:, :], AF.Identity,
                    bias=bels_t[:, m:m + 1], scale=els_t[:, m:m + 1],
                    rd=[rr(pb[bank]), rr(els_t), rr(bels_t)], wr=[rr(csA)])
                if m >= 1:
                    for j, slot in enumerate((s2, s3, s4, s5)):
                        ln_stat_step(4 * (m - 1) + j, NCTX,
                                     slot[:, (m - 1) * T:m * T],
                                     rr(slot, m - 1))
                tt(csA[:, :], csA[:, :], posc_ap, ALU.mult,
                   rd=[rr(csA), rr(posc_b)], wr=[rr(csA)])
                scan(csB[:, :], csA[:, :],
                     0.0 if first_in_seq else phc[:, m:m + 1],
                     rd=[rr(csA)] + ([] if first_in_seq else [rr(phc, m)]),
                     wr=[rr(csB)])
                ts(qi32[:, :], csB[:, :], INV_2PI, 0.5, ALU.mult, ALU.add,
                   rd=[rr(csB)], wr=[rr(qi32)])
                vcp(csC[:, :], qi32[:, :], rd=[rr(qi32)], wr=[rr(csC)])
                stt(csD[:, :], csC[:, :], -CW1, csB[:, :], ALU.mult, ALU.add,
                    rd=[rr(csC), rr(csB)], wr=[rr(csD)])
                vcp(phc[:, m:m + 1], csB[:, T - 1:T],
                    rd=[rr(csB)], wr=[rr(phc, m)])
                stt(csD[:, :], csC[:, :], -CW2, csD[:, :], ALU.mult, ALU.add,
                    rd=[rr(csC), rr(csD)], wr=[rr(csD)])
                ts(csC[:, :], csD[:, :], -PI_B, None, ALU.is_lt,
                   rd=[rr(csD)], wr=[rr(csC)])
                stt(csB[:, :], csC[:, :], CW1, csD[:, :], ALU.mult, ALU.add,
                    rd=[rr(csC), rr(csD)], wr=[rr(csB)])
                ts(csC[:, :], csB[:, :], PIH_HI, None, ALU.is_gt,
                   rd=[rr(csB)], wr=[rr(csC)])
                stt(csD[:, :], csC[:, :], -CW1, csB[:, :], ALU.mult, ALU.add,
                    rd=[rr(csC), rr(csB)], wr=[rr(csD)])
                ts(csD[:, :], csD[:, :], PIH_LO, PIH_HI, ALU.add, ALU.add,
                   rd=[rr(csD)], wr=[rr(csD)])
                ts(csB[:, :], csB[:, :], -PI_B, PI_B, ALU.max, ALU.min,
                   rd=[rr(csB)], wr=[rr(csB)])
                ts(csD[:, :], csD[:, :], -PI_B, PI_B, ALU.max, ALU.min,
                   rd=[rr(csD)], wr=[rr(csD)])
                act(sinc[:, :], csB[:, :], AF.Sin, rd=[rr(csB)], wr=[rr(sinc)])
                act(cosc[:, :], csD[:, :], AF.Sin, rd=[rr(csD)], wr=[rr(cosc)])
                xc = s1[:, m * T:(m + 1) * T]
                gtt(s2[:, m * T:(m + 1) * T], xc, cosc[:, :], ALU.mult,
                    rd=[rr(s1, m), rr(cosc)], wr=[rr(s2, m)])
                gtt(s3[:, m * T:(m + 1) * T], xc, sinc[:, :], ALU.mult,
                    rd=[rr(s1, m), rr(sinc)], wr=[rr(s3, m)])
                scan(csA[:, :], s2[:, m * T:(m + 1) * T],
                     0.0 if first_in_seq else mrc[:, m:m + 1],
                     rd=[rr(s2, m)] + ([] if first_in_seq else [rr(mrc, m)]),
                     wr=[rr(csA)])
                scan(csB[:, :], s3[:, m * T:(m + 1) * T],
                     0.0 if first_in_seq else mic[:, m:m + 1],
                     rd=[rr(s3, m)] + ([] if first_in_seq else [rr(mic, m)]),
                     wr=[rr(csB)])
                tt(csC[:, :], csA[:, :], ivp_ap, ALU.mult,
                   rd=[rr(csA), rr(ivp_b)], wr=[rr(csC)])
                tt(csD[:, :], csB[:, :], ivp_ap, ALU.mult,
                   rd=[rr(csB), rr(ivp_b)], wr=[rr(csD)])
                vcp(mrc[:, m:m + 1], csA[:, T - 1:T],
                    rd=[rr(csA)], wr=[rr(mrc, m)])
                vcp(mic[:, m:m + 1], csB[:, T - 1:T],
                    rd=[rr(csB)], wr=[rr(mic, m)])
                tt(csA[:, :], csC[:, :], cosc[:, :], ALU.mult,
                   rd=[rr(csC), rr(cosc)], wr=[rr(csA)])
                tt(csB[:, :], csD[:, :], sinc[:, :], ALU.mult,
                   rd=[rr(csD), rr(sinc)], wr=[rr(csB)])
                tt(s4[:, m * T:(m + 1) * T], csA[:, :], csB[:, :], ALU.add,
                   rd=[rr(csA), rr(csB)], wr=[rr(s4, m)])
                E.emit("gp", lambda e: e.tensor_tensor(
                    cosc[:, :], csD[:, :], cosc[:, :], ALU.mult),
                       [rr(csD), rr(cosc)], [rr(cosc)])
                E.emit("gp", lambda e: e.tensor_tensor(
                    sinc[:, :], csC[:, :], sinc[:, :], ALU.mult),
                       [rr(csC), rr(sinc)], [rr(sinc)])
                E.emit("gp", lambda e, m=m: e.tensor_tensor(
                    s5[:, m * T:(m + 1) * T], cosc[:, :], sinc[:, :],
                    ALU.subtract),
                       [rr(cosc), rr(sinc)], [rr(s5, m)])
                drain(4)

            for nm, slot in (("cm_r", s2), ("cm_i", s3), ("ret_r", s4),
                             ("ret_i", s5)):
                if nm in dbg_hs:
                    dma("sy", dbg_hs[nm][t, :, :], slot[:, :],
                        rd=rrs(slot, range(NCH)))

            def ctx_ap(cc):
                slot = (s2, s3, s4, s5)[cc // 8]
                m = cc % 8
                return slot[:, m * T:(m + 1) * T], rr(slot, m)

            for j, slot in enumerate((s2, s3, s4, s5)):
                ln_stat_step(4 * 7 + j, NCTX, slot[:, 7 * T:8 * T],
                             rr(slot, 7))
            ln_finalize(float(CTX))
            for cc in range(NCTX):
                a, res = ctx_ap(cc)
                buf = csA if cc % 2 == 0 else csB
                gtt(buf[:, :], a, rstd_b[:, :], ALU.mult,
                    rd=[res, rr(rstd_b)], wr=[rr(buf)])
                tt(buf[:, :], buf[:, :], nmr_b[:, :], ALU.subtract,
                   rd=[rr(buf), rr(nmr_b)], wr=[rr(buf)])
                act(hnb[:, cc * T:(cc + 1) * T], buf[:, :], AF.Identity,
                    scale=lng_t[:, cc:cc + 1], bias=lnb_t[:, cc:cc + 1],
                    rd=[rr(buf), rr(lng_t), rr(lnb_t)], wr=[rr(hnb, cc)])
                drain(2)

            for m in range(NH1):
                bank = m % 2
                for half in range(2):
                    slab, sres = load_big(w1T_h, m, half * 16, 16)
                    for kk in range(16):
                        k = half * 16 + kk
                        mm(pb[bank][:, :], slab[:, kk * 128:(kk + 1) * 128],
                           hnb[:, k * T:(k + 1) * T], k == 0, k == NCTX - 1,
                           rd=[sres, rr(hnb, k)], wr=[rr(pb[bank])])
                    drain(6)
                act(glb[:, m * T:(m + 1) * T], pb[bank][:, :], AF.Gelu,
                    bias=b1_t[:, m:m + 1],
                    rd=[rr(pb[bank]), rr(b1_t)], wr=[rr(glb, m)])

            for m in range(NCH):
                slab, sres = load_big(w2T_h, m, 0, NH1)
                bank = m % 2
                for k in range(NH1):
                    mm(pb[bank][:, :], slab[:, k * 128:(k + 1) * 128],
                       glb[:, k * T:(k + 1) * T], k == 0, k == NH1 - 1,
                       rd=[sres, rr(glb, k)], wr=[rr(pb[bank])])
                act(csA[:, :], pb[bank][:, :], AF.Identity, bias=b2_t[:, m:m + 1],
                    rd=[rr(pb[bank]), rr(b2_t)], wr=[rr(csA)])
                stt(s2[:, m * T:(m + 1) * T], s1[:, m * T:(m + 1) * T],
                    2.0, csA[:, :], ALU.mult, ALU.add,
                    rd=[rr(s1, m), rr(csA)], wr=[rr(s2, m)])
                ln_stat_step(m, NCH, s2[:, m * T:(m + 1) * T], rr(s2, m),
                             stage="hnb")
                drain(8)

            if "y1" in dbg_hs:
                dma("sy", dbg_hs["y1"][t, :, :], s2[:, :], rd=rrs(s2, range(NCH)))

            ln_finalize(float(D))
            for m in range(NCH):
                buf = csA if m % 2 == 0 else csB
                gtt(buf[:, :], s2[:, m * T:(m + 1) * T], rstd_b[:, :], ALU.mult,
                    rd=[rr(s2, m), rr(rstd_b)], wr=[rr(buf)])
                tt(s3[:, m * T:(m + 1) * T], buf[:, :], nmr_b[:, :],
                   ALU.subtract, rd=[rr(buf), rr(nmr_b)], wr=[rr(s3, m)])
                rowmax_acc(m, s3[:, m * T:(m + 1) * T], rr(s3, m))

            if "x2" in dbg_hs:
                dma("sy", dbg_hs["x2"][t, :, :], s3[:, :], rd=rrs(s3, range(NCH)))

            ensure("wd")
            rowmax_finish(tms[:, :], rr(tms))
            act(tmm[:, 0:NB], tms[:, :], AF.Ln, rd=[rr(tms)], wr=[rr(tmm)])
            act(tmc[:, 0:NB], tmm[:, 0:NB], AF.Exp, scale=-0.5, bias=cst[:, 1:2],
                rd=[rr(tmm), rr(cst)], wr=[rr(tmc)])
            act(tmm[:, 0:NB], tms[:, :], AF.Sqrt, rd=[rr(tms)], wr=[rr(tmm)])
            ts(tmc[:, NB:2 * NB], tmm[:, 0:NB], mg127_t[:, 0:1], None, ALU.mult,
               rd=[rr(tmm), rr(mg127_t)], wr=[rr(tmc)])
            bcast_pair(2, s1q_b[:, :], rr(s1q_b), cg_b[:, :], rr(cg_b))
            for m in range(NCH):
                gtt(csB[:, :], s3[:, m * T:(m + 1) * T], s1q_b[:, :],
                    ALU.mult, rd=[rr(s3, m), rr(s1q_b)], wr=[rr(csB)])
                round_to_bf16(csB[:, :], rr(csB),
                              xq[:, m * T:(m + 1) * T], rr(glb, m))
                drain(2)

            def hf_ap(o):
                slot = (s1, s2, s4, s5)[o // 8]
                m = o % 8
                return slot[:, m * T:(m + 1) * T], rr(slot, m)

            ensure("wu")
            for o in range(NIN):
                gslab, gres = load_small(sWg_h, o, D)
                uslab, ures = load_small(sWu_h, o, D)
                gbank, ubank = (0, 1) if o % 2 == 0 else (2, 3)
                gst = gps0 if o % 2 == 0 else gps1
                ust = ups0 if o % 2 == 0 else ups1
                for k in range(NCH):
                    mm(pb[gbank][:, :], gslab[:, k * 128:(k + 1) * 128],
                       xq[:, k * T:(k + 1) * T], k == 0, k == NCH - 1,
                       rd=[gres, rr(glb, k)], wr=[rr(pb[gbank])])
                for k in range(NCH):
                    mm(pb[ubank][:, :], uslab[:, k * 128:(k + 1) * 128],
                       xq[:, k * T:(k + 1) * T], k == 0, k == NCH - 1,
                       rd=[ures, rr(glb, k)], wr=[rr(pb[ubank])])
                act(gst[:, :], pb[gbank][:, :], AF.Identity,
                    rd=[rr(pb[gbank])], wr=[rr(gst)])
                act(ust[:, :], pb[ubank][:, :], AF.Identity,
                    rd=[rr(pb[ubank])], wr=[rr(ust)])
                tt(csA[:, :], gst[:, :], cg_b[:, :], ALU.mult,
                   rd=[rr(gst), rr(cg_b)], wr=[rr(csA)])
                act(csB[:, :], csA[:, :], AF.Silu, rd=[rr(csA)], wr=[rr(csB)])
                ha, hres = hf_ap(o)
                E.emit("gp", lambda e, ha=ha, u=ust:
                       e.tensor_tensor(ha, csB[:, :], u[:, :], ALU.mult),
                       [rr(csB), rr(ust)], [hres])
                rowmax_acc(o, ha, hres)
                drain(4)

            rowmax_finish(tml[:, 0:NB], rr(tml))
            act(tmm[:, 0:NB], tml[:, 0:NB], AF.Ln, rd=[rr(tml)], wr=[rr(tmm)])
            act(tmc[:, 0:NB], tmm[:, 0:NB], AF.Exp, scale=-0.5, bias=cst[:, 1:2],
                rd=[rr(tmm), rr(cst)], wr=[rr(tmc)])
            act(tmq[:, 0:NB], tml[:, 0:NB], AF.Sqrt, rd=[rr(tml)], wr=[rr(tmq)])
            act(tmm[:, 0:NB], tms[:, :], AF.Sqrt, rd=[rr(tms)], wr=[rr(tmm)])
            tt(tmc[:, NB:2 * NB], tmq[:, 0:NB], tmm[:, 0:NB], ALU.mult,
               rd=[rr(tmq), rr(tmm)], wr=[rr(tmc)])
            ts(tmc[:, NB:2 * NB], tmc[:, NB:2 * NB], mud_t[:, 0:1], None, ALU.mult,
               rd=[rr(tmc), rr(mud_t)], wr=[rr(tmc)])
            bcast_pair(2, s2q_b[:, :], rr(s2q_b), cfin_b[:, :], rr(cfin_b))
            for o in range(NIN):
                ha, hres = hf_ap(o)
                gtt(csB[:, :], ha, s2q_b[:, :], ALU.mult,
                    rd=[hres, rr(s2q_b)], wr=[rr(csB)])
                round_to_bf16(csB[:, :], rr(csB),
                              hnb[:, o * T:(o + 1) * T], rr(hnb, o))
                drain(1)

            if "scales" in dbg_hs:
                for i, (bt, res) in enumerate([(s1q_b, rr(s1q_b)), (cg_b, rr(cg_b)),
                                               (s2q_b, rr(s2q_b)), (cfin_b, rr(cfin_b))]):
                    dma("sy", dbg_hs["scales"][t, :, i * T:(i + 1) * T], bt[:, :],
                        rd=[res])
            if "hf0" in dbg_hs:
                for i, slot in enumerate((s1, s2, s4, s5)):
                    dma("sy", dbg_hs["hf%d" % i][t, :, :], slot[:, :],
                        rd=rrs(slot, range(NCH)))

            for k in range(NIN):
                slab, sres = load_small(sWd_h, k, D)
                for m in range(NCH):
                    mm(pb[m][:, :], slab[:, m * 128:(m + 1) * 128],
                       hnb[:, k * T:(k + 1) * T], k == 0, k == NIN - 1,
                       rd=[sres, rr(hnb, k)], wr=[rr(pb[m])])
            evord = [6, 7, 0, 1, 2, 3, 4, 5]
            for p, m in enumerate(evord):
                st = gps0 if p % 2 == 0 else gps1
                act(st[:, :], pb[m][:, :], AF.Identity,
                    rd=[rr(pb[m])], wr=[rr(st)])
                gtt(csA[:, :], st[:, :], cfin_b[:, :], ALU.mult,
                    rd=[rr(st), rr(cfin_b)], wr=[rr(csA)])
                tt(s4[:, m * T:(m + 1) * T], csA[:, :],
                   s3[:, m * T:(m + 1) * T], ALU.add,
                   rd=[rr(csA), rr(s3, m)], wr=[rr(s4, m)])
                if p >= 2:
                    ms = evord[p - 2]
                    ln_stat_step(p - 2, NCH, s4[:, ms * T:(ms + 1) * T],
                                 rr(s4, ms), stage="hnb", banks=(6, 7))
                drain(2)
            for p in (6, 7):
                ms = evord[p]
                ln_stat_step(p, NCH, s4[:, ms * T:(ms + 1) * T],
                             rr(s4, ms), stage="hnb", banks=(6, 7))

            if "y2" in dbg_hs:
                dma("sy", dbg_hs["y2"][t, :, :], s4[:, :], rd=rrs(s4, range(NCH)))

            ln_finalize(float(D), banks=(6, 7))
            for m in range(NCH):
                buf = csA if m % 2 == 0 else csB
                gtt(buf[:, :], s4[:, m * T:(m + 1) * T], rstd_b[:, :], ALU.mult,
                    rd=[rr(s4, m), rr(rstd_b)], wr=[rr(buf)])
                tt(s5[:, m * T:(m + 1) * T], buf[:, :], nmr_b[:, :],
                   ALU.subtract, rd=[rr(buf), rr(nmr_b)], wr=[rr(s5, m)])
                drain(2)

            for hh in range(2):
                for tc in range(2):
                    for f in range(NCH):
                        bank = 6 if f < 4 else 7
                        E.emit("te", lambda e, tc=tc, f=f, bank=bank, hh=hh:
                               e.transpose(pb[bank][:, (f % 4) * 128:(f % 4 + 1) * 128],
                                           s5[:, f * T + (hh * 2 + tc) * 128:
                                              f * T + (hh * 2 + tc + 1) * 128],
                                           identt[:, :]),
                               [rr(s5, f), rr(identt)], [rr(pb[bank])])
                    act(x_tm[:, tc * D:tc * D + 512], pb[6][:, :],
                        AF.Identity, rd=[rr(pb[6])], wr=[rr(x_tm, tc)])
                    act(x_tm[:, tc * D + 512:(tc + 1) * D], pb[7][:, :],
                        AF.Identity, rd=[rr(pb[7])], wr=[rr(x_tm, tc)])
                dma("sy",
                    out_h[t * T + hh * 256:t * T + (hh + 1) * 256, :]
                    .rearrange("(c p) d -> p c d", p=128),
                    x_tm[:].rearrange("p (c d) -> p c d", d=D),
                    rd=rrs(x_tm, range(2)), wr=[("out", t, hh)])

        for t in range(NT):
            emit_tile(t)

        E.build(block, sems, dma_sems)

    return nc


_prog_cache = {}


def _get_prog():
    key = tuple(_DBG_WANT)
    if key not in _prog_cache:
        _prog_cache[key] = build_program()
    return _prog_cache[key]


def _pp(vec, nch):
    return np.ascontiguousarray(np.asarray(vec, np.float32).reshape(nch, 128).T)


def kernel(L0_attn_x, L0_attn_W_omega, L0_attn_b_omega, L0_attn_log_scale,
           L0_attn_ln_gamma, L0_attn_ln_beta, L0_attn_W1, L0_attn_b1,
           L0_attn_W2, L0_attn_b2, L0_ffn_w_gate, L0_ffn_w_up, L0_ffn_w_down):
    x = np.asarray(L0_attn_x, np.float32)
    shared = dict(
        wOmT=np.ascontiguousarray(np.asarray(L0_attn_W_omega, np.float32).T),
        w1T=np.ascontiguousarray(np.asarray(L0_attn_W1, np.float32).T),
        w2T=np.ascontiguousarray(np.asarray(L0_attn_W2, np.float32).T),
        wgT=np.ascontiguousarray(np.asarray(L0_ffn_w_gate, np.float32).T),
        wuT=np.ascontiguousarray(np.asarray(L0_ffn_w_up, np.float32).T),
        wdT=np.ascontiguousarray(np.asarray(L0_ffn_w_down, np.float32).T),
        bom=_pp(L0_attn_b_omega, NCH),
        lsc=_pp(L0_attn_log_scale, NCH),
        lng=_pp(L0_attn_ln_gamma, NCTX),
        lnb=_pp(L0_attn_ln_beta, NCTX),
        b1v=_pp(L0_attn_b1, NH1),
        b2v=_pp(L0_attn_b2, NCH),
        ident=np.eye(128, dtype=np.float32),
        ones32=np.ones((128, 128), np.float32),
    )
    in_maps = []
    for c in range(NCORE):
        m = dict(shared)
        m["x"] = np.ascontiguousarray(
            x[c * SEQ_PER_CORE:(c + 1) * SEQ_PER_CORE].reshape(TOK, D))
        in_maps.append(m)
    nc = _get_prog()
    res = run_bass_kernel_spmd(nc, in_maps, core_ids=list(range(NCORE)))
    out = np.empty((B, S, D), np.float32)
    for c in range(NCORE):
        out[c * SEQ_PER_CORE:(c + 1) * SEQ_PER_CORE] = \
            np.asarray(res.results[c]["out"], np.float32).reshape(
                SEQ_PER_CORE, S, D)
    return out


# version 22
# speedup vs baseline: 1.0703x; 1.0703x over previous
import math
import numpy as np
import concourse.bass as bass
import concourse.mybir as mybir
from concourse.bass_utils import run_bass_kernel_spmd

F32 = mybir.dt.float32
I32 = mybir.dt.int32
BF16 = mybir.dt.bfloat16
AF = mybir.ActivationFunctionType
ALU = mybir.AluOpType
AX = mybir.AxisListType.X

B, S, D = 16, 2048, 1024
NCORE = 8
SEQ_PER_CORE = B // NCORE
TOK = SEQ_PER_CORE * S
T = 512
NT = TOK // T
TPS = S // T
NCH = D // 128
CTX = 4 * D
H1 = 2 * D
INNER = 4 * D
NCTX = CTX // 128
NH1 = H1 // 128
NIN = INNER // 128
NB = T // 128
LN_EPS = 1e-5

TWO_PI = 2.0 * math.pi
INV_2PI = float(np.float32(1.0 / TWO_PI))
CW1 = float(np.float32(TWO_PI))
CW2 = float(np.float32(TWO_PI - CW1))
PI_B = float(np.float32(np.nextafter(np.float32(math.pi), np.float32(0.0))))
PIH_HI = float(np.float32(math.pi / 2))
PIH_LO = float(np.float32(math.pi / 2 - PIH_HI))
LN127 = float(np.float32(math.log(127.0)))

_DBG_WANT = []
_TRUNC = None


class Emitter:
    ENGINES = ("te", "ve", "ac", "sy", "gp")
    NDS = 8

    def __init__(self):
        self.steps = []

    def emit(self, eng, fn, reads=(), writes=(), dma=False, async_p=False,
             no_inc=False):
        self.steps.append((eng, fn, tuple(reads), tuple(writes), dma, async_p,
                           no_inc))

    def dma(self, queue, fn, reads=(), writes=()):
        self.emit(queue, fn, reads, writes, dma=True)

    def build(self, block, sems, dma_sems):
        if _TRUNC is not None:
            self.steps = self.steps[:_TRUNC]
        steps = self.steps
        n = len(steps)
        NDS = self.NDS
        last_write = {}
        readers = {}
        deps = [set() for _ in range(n)]
        pre_dcnt = {"sy": 0, "ac": 0, "gp": 0}
        war_key = [None] * n
        for i, (eng, fn, reads, writes, is_dma, _, _) in enumerate(steps):
            if is_dma:
                war_key[i] = (eng, "dma", pre_dcnt[eng] % NDS)
                pre_dcnt[eng] += 1
            else:
                war_key[i] = (eng,)
            for r in reads:
                if r in last_write:
                    deps[i].add(last_write[r])
            for w in writes:
                if w in last_write:
                    deps[i].add(last_write[w])
                by_stream = {}
                for rd in readers.get(w, ()):
                    by_stream[war_key[rd]] = rd
                deps[i].update(by_stream.values())
            for r in reads:
                readers.setdefault(r, []).append(i)
            for w in writes:
                last_write[w] = i
                readers[w] = []
        flagged = [False] * n
        for i, dset in enumerate(deps):
            for j in dset:
                if steps[j][4]:
                    continue
                if steps[j][6]:
                    continue
                if (steps[j][0] != steps[i][0] or steps[j][5]
                        or steps[j][0] in ("ac", "ve", "gp")):
                    flagged[j] = True
        NDS = self.NDS
        seqno = [0] * n
        dslot = [0] * n
        cnt = {e: 0 for e in self.ENGINES}
        dcnt = {"sy": 0, "ac": 0, "gp": 0}
        slot_cnt = {}
        for i, st in enumerate(steps):
            if st[4]:
                s = dcnt[st[0]] % NDS
                dcnt[st[0]] += 1
                dslot[i] = s
                slot_cnt[(st[0], s)] = slot_cnt.get((st[0], s), 0) + 1
                seqno[i] = slot_cnt[(st[0], s)]
            elif flagged[i]:
                cnt[st[0]] += 1
                seqno[i] = cnt[st[0]]
        per_eng = {e: [] for e in self.ENGINES}
        for i, st in enumerate(steps):
            per_eng[st[0]].append(i)

        def run_engine(eng_name, eng):
            waited = {}

            def wait(key, sem, val):
                if waited.get(key, 0) >= val:
                    return
                eng.wait_ge(sem, val)
                waited[key] = val

            for i in per_eng[eng_name]:
                _, fn, reads, writes, is_dma, _asy, _ni = steps[i]
                need = {}
                for j in deps[i]:
                    src = steps[j][0]
                    if steps[j][4]:
                        key = (src, "dma", dslot[j])
                        val = seqno[j] * 16
                        sem = dma_sems[src][dslot[j]]
                    else:
                        if src == eng_name and not flagged[j]:
                            continue
                        key, val, sem = (src, "sem"), seqno[j], sems[src]
                    if val > need.get(key, (0, None))[0]:
                        need[key] = (val, sem)
                for key, (val, sem) in sorted(need.items(), key=str):
                    wait(key, sem, val)
                if is_dma:
                    s = dslot[i]
                    if seqno[i] > 1:
                        wait((eng_name, "dma", s), dma_sems[eng_name][s],
                             (seqno[i] - 1) * 16)
                inst = fn(eng)
                if is_dma:
                    inst.then_inc(dma_sems[eng_name][dslot[i]], 16)
                elif flagged[i]:
                    inst.then_inc(sems[eng_name], 1)
            if eng_name == "sy":
                for q in ("sy", "ac", "gp"):
                    for s in range(NDS):
                        tot = slot_cnt.get((q, s), 0)
                        if tot > 0:
                            eng.wait_ge(dma_sems[q][s], tot * 16)

        @block.tensor
        def _(tensor):
            run_engine("te", tensor)

        @block.vector
        def _(vector):
            run_engine("ve", vector)

        @block.scalar
        def _(scalar):
            run_engine("ac", scalar)

        @block.sync
        def _(sync):
            run_engine("sy", sync)

        @block.gpsimd
        def _(gpsimd):
            run_engine("gp", gpsimd)


def build_program():
    nc = bass.Bass("TRN2", target_bir_lowering=False,
                   detect_race_conditions=False)

    x_h = nc.dram_tensor("x", [TOK, D], F32, kind="ExternalInput").ap()
    wOmT_h = nc.dram_tensor("wOmT", [D, D], F32, kind="ExternalInput").ap()
    w1T_h = nc.dram_tensor("w1T", [CTX, H1], F32, kind="ExternalInput").ap()
    w2T_h = nc.dram_tensor("w2T", [H1, D], F32, kind="ExternalInput").ap()
    wgT_h = nc.dram_tensor("wgT", [D, INNER], F32, kind="ExternalInput").ap()
    wuT_h = nc.dram_tensor("wuT", [D, INNER], F32, kind="ExternalInput").ap()
    wdT_h = nc.dram_tensor("wdT", [INNER, D], F32, kind="ExternalInput").ap()
    bom_h = nc.dram_tensor("bom", [128, NCH], F32, kind="ExternalInput").ap()
    lsc_h = nc.dram_tensor("lsc", [128, NCH], F32, kind="ExternalInput").ap()
    lng_h = nc.dram_tensor("lng", [128, NCTX], F32, kind="ExternalInput").ap()
    lnb_h = nc.dram_tensor("lnb", [128, NCTX], F32, kind="ExternalInput").ap()
    b1_h = nc.dram_tensor("b1v", [128, NH1], F32, kind="ExternalInput").ap()
    b2_h = nc.dram_tensor("b2v", [128, NCH], F32, kind="ExternalInput").ap()
    ident_h = nc.dram_tensor("ident", [128, 128], F32, kind="ExternalInput").ap()
    ones32_h = nc.dram_tensor("ones32", [128, 128], F32, kind="ExternalInput").ap()
    out_h = nc.dram_tensor("out", [TOK, D], F32, kind="ExternalOutput").ap()

    sOm_h = nc.dram_tensor("sOm", [NCH, 128, D], BF16, kind="Internal").ap()
    sW1_h = nc.dram_tensor("sW1", [NH1, 128, CTX], BF16, kind="Internal").ap()
    sW2_h = nc.dram_tensor("sW2", [NCH, 128, H1], BF16, kind="Internal").ap()
    sWg_h = nc.dram_tensor("sWg", [NIN, 128, D], BF16, kind="Internal").ap()
    sWu_h = nc.dram_tensor("sWu", [NIN, 128, D], BF16, kind="Internal").ap()
    sWd_h = nc.dram_tensor("sWd", [NIN, 128, D], BF16, kind="Internal").ap()

    dbg_hs = {}
    for name in _DBG_WANT:
        dbg_hs[name] = nc.dram_tensor("dbg_" + name, [NT, 128, NCH * T], F32,
                                      kind="ExternalOutput").ap()

    from contextlib import ExitStack
    es = ExitStack()
    with es:
        def sbt(name, shape, dt=F32):
            return es.enter_context(nc.sbuf_tensor(name, shape, dt))

        s1 = sbt("s1", [128, NCH * T]); s2 = sbt("s2", [128, NCH * T])
        s3 = sbt("s3", [128, NCH * T]); s4 = sbt("s4", [128, NCH * T])
        s5 = sbt("s5", [128, NCH * T])
        x_tm = sbt("x_tm", [128, 2 * D])
        hnb = sbt("hnb", [128, NCTX * T], BF16)
        glb = sbt("glb", [128, NH1 * T], BF16)
        wb = [sbt(f"wb{i}", [128, 2048], BF16) for i in range(3)]
        wsm = [sbt(f"wsm{i}", [128, 1024], BF16) for i in range(5)]
        csA = sbt("csA", [128, T]); csB = sbt("csB", [128, T])

        csC = sbt("csC", [128, T]); csD = sbt("csD", [128, T])
        sinc = sbt("sinc", [128, T]); cosc = sbt("cosc", [128, T])
        qi32 = sbt("qi32", [128, T], I32)
        sbA = sbt("sbA", [128, T], BF16); sbB = sbt("sbB", [128, T], BF16)
        mu_b = sbt("mu_b", [128, T]); rstd_b = sbt("rstd_b", [128, T])
        gps0 = mu_b; gps1 = rstd_b
        nmr_b = sbt("nmr_b", [128, T]); aux_b = sbt("aux_b", [128, T])
        ups0 = nmr_b; ups1 = aux_b
        aux2_b = csD
        s1q_b = sbt("s1q_b", [128, T]); cg_b = sbt("cg_b", [128, T])
        s2q_b = sbt("s2q_b", [128, T]); cfin_b = sbt("cfin_b", [128, T])
        posc_b = sbt("posc_b", [128, T]); ivp_b = sbt("ivp_b", [128, T])
        prow = sbt("prow", [1, T])
        zrow = sbt("zrow", [128, T + 2 * NB * 128 + 4])
        tmm = sbt("tmm", [128, NB]); tms = sbt("tms", [128, NB])
        tmq = sbt("tmq", [128, NB])
        tml = sbt("tml", [128, NB])
        tmc = sbt("tmc", [128, 2 * NB])
        rowst = sbt("rowst", [2 * NB, 128])
        identt = sbt("identt", [128, 128]); onest = sbt("onest", [128, 128])
        onebf = sbt("onebf", [128, 128], BF16)
        bom_t = sbt("bom_t", [128, NCH]); lsc_t = sbt("lsc_t", [128, NCH])
        els_t = sbt("els_t", [128, NCH]); bels_t = sbt("bels_t", [128, NCH])
        lng_t = sbt("lng_t", [128, NCTX]); lnb_t = sbt("lnb_t", [128, NCTX])
        b1_t = sbt("b1_t", [128, NH1]); b2_t = sbt("b2_t", [128, NCH])
        mg_t = sbt("mg_t", [128, 1]); mu_t = sbt("mu_t", [128, 1])
        md_t = sbt("md_t", [128, 1])
        sg_t = sbt("sg_t", [128, 1]); su_t = sbt("su_t", [128, 1])
        sd_t = sbt("sd_t", [128, 1])
        mud_t = sbt("mud_t", [128, 1]); mg127_t = sbt("mg127_t", [128, 1])
        accp = sbt("accp", [128, 1]); accw = sbt("accw", [128, 64])
        thrp_t = sbt("thrp_t", [128, 1]); thrn_t = sbt("thrn_t", [128, 1])
        sbC = sbt("sbC", [128, T], BF16); sbD = sbt("sbD", [128, T], BF16)
        tmn1 = sbt("tmn1", [128, 1]); tmn2 = sbt("tmn2", [128, 1])
        cst = sbt("cst", [128, 3])
        phc = sbt("phc", [128, NCH]); mrc = sbt("mrc", [128, NCH])
        mic = sbt("mic", [128, NCH])

        pb = [es.enter_context(nc.psum_tensor(f"pb{i}", [128, T], F32))
              for i in range(8)]

        sems = {e: es.enter_context(nc.semaphore("sem_" + e))
                for e in Emitter.ENGINES}
        dma_sems = {q: [es.enter_context(nc.semaphore(f"dsem_{q}{i}"))
                        for i in range(Emitter.NDS)]
                    for q in ("sy", "ac", "gp")}
        block = es.enter_context(nc.Block())

        E = Emitter()

        def rr(h, c=0):
            return (id(h), c)

        def rrs(h, cs):
            return [(id(h), c) for c in cs]

        def ts(out, in0, sc1, sc2, op0, op1=None, rd=(), wr=()):
            if op1 is None:
                E.emit("ve", lambda e, o=out, i=in0, a=sc1, p0=op0:
                       e.tensor_scalar(o, i, a, None, p0), rd, wr)
            else:
                E.emit("ve", lambda e, o=out, i=in0, a=sc1, b=sc2, p0=op0, p1=op1:
                       e.tensor_scalar(o, i, a, b, p0, p1), rd, wr)

        def tt(out, in0, in1, op, rd=(), wr=()):
            E.emit("ve", lambda e, o=out, i=in0, j=in1, p=op:
                   e.tensor_tensor(o, i, j, p), rd, wr)

        def gtt(out, in0, in1, op, rd=(), wr=()):
            E.emit("gp", lambda e, o=out, i=in0, j=in1, p=op:
                   e.tensor_tensor(o, i, j, p), rd, wr)

        def stt(out, in0, sc, in1, op0, op1, rd=(), wr=()):
            E.emit("ve", lambda e, o=out, i=in0, s=sc, j=in1, p0=op0, p1=op1:
                   e.scalar_tensor_tensor(o, i, s, j, p0, p1), rd, wr)

        def act(out, in_, func, bias=0.0, scale=1.0, accum=None, rd=(), wr=()):
            E.emit("ac", lambda e, o=out, i=in_, f=func, b=bias, s=scale, a=accum:
                   e.activation(o, i, f, bias=b, scale=s, accum_out=a), rd, wr)

        def vcp(out, in_, rd=(), wr=()):
            E.emit("ve", lambda e, o=out, i=in_: e.tensor_copy(o, i), rd, wr)

        def mm(out, lhsT, rhs, start, stop, rd=(), wr=()):
            E.emit("te", lambda e, o=out, l=lhsT, r=rhs, a=start, b=stop:
                   e.matmul(o, l, r, start=a, stop=b), rd, wr)

        def tr(out, in_, rd=(), wr=()):
            E.emit("te", lambda e, o=out, i=in_:
                   e.transpose(o, i, identt[:, :]), rd, wr)

        def dma(q, out, in_, rd=(), wr=()):
            E.dma(q, lambda e, o=out, i=in_: e.dma_start(out=o, in_=i), rd, wr)

        def scan(out, data0, init, rd=(), wr=()):
            E.emit("ve", lambda e, o=out, d=data0, v=init:
                   e.tensor_tensor_scan(o, d, d, v, ALU.add, ALU.bypass),
                   rd, wr, no_inc=True)

        dma("sy", identt[:, :], ident_h, wr=[rr(identt)])
        dma("sy", onest[:, :], ones32_h, wr=[rr(onest)])
        act(onebf[:, :], onest[:, :], AF.Identity, rd=[rr(onest)], wr=[rr(onebf)])
        dma("sy", bom_t[:, :], bom_h, wr=[rr(bom_t)])
        dma("sy", lsc_t[:, :], lsc_h, wr=[rr(lsc_t)])
        dma("sy", lng_t[:, :], lng_h, wr=[rr(lng_t)])
        dma("sy", lnb_t[:, :], lnb_h, wr=[rr(lnb_t)])
        dma("sy", b1_t[:, :], b1_h, wr=[rr(b1_t)])
        dma("sy", b2_t[:, :], b2_h, wr=[rr(b2_t)])
        act(els_t[:, :], lsc_t[:, :], AF.Exp, rd=[rr(lsc_t)], wr=[rr(els_t)])
        tt(bels_t[:, :], bom_t[:, :], els_t[:, :], ALU.mult,
           rd=[rr(bom_t), rr(els_t)], wr=[rr(bels_t)])

        E.emit("ve", lambda e: e.memset(prow[0:1, :], 1.0), (), [rr(prow)], async_p=True)
        E.emit("ve", lambda e: e.memset(zrow[:, :], 0.0), (),
               [rr(zrow, c) for c in range(5)], async_p=True)
        E.emit("ve", lambda e: e.memset(cst[:, 0:1], LN_EPS), (), [rr(cst)], async_p=True)
        E.emit("ve", lambda e: e.memset(cst[:, 1:2], LN127), (), [rr(cst)], async_p=True)
        E.emit("ve", lambda e: e.memset(cst[:, 2:3], 0.5), (), [rr(cst)], async_p=True)

        def cast_gen(src_h, dst_h, n_m, n_k, tag_h):
            for m in range(n_m):
                for kh in range((n_k + 15) // 16):
                    k0, k1 = kh * 16, min(n_k, (kh + 1) * 16)
                    src = src_h[k0 * 128:k1 * 128, m * 128:(m + 1) * 128] \
                        .rearrange("(k p) c -> p k c", p=128)
                    dst = dst_h[m, :, k0 * 128:k1 * 128] \
                        .rearrange("p (k c) -> p k c", c=128)
                    dma("gp", dst, src, wr=[rr(tag_h, m)])
                    yield

        tern_slabs = [(s1q_b[:, :], rr(s1q_b)), (s2q_b[:, :], rr(s2q_b)),
                      (x_tm[:, 0:512], rr(x_tm, 0)),
                      (x_tm[:, 1024:1536], rr(x_tm, 1))]
        tern_outs = [(cg_b[:, :], rr(cg_b)), (cfin_b[:, :], rr(cfin_b))]
        ZMCOL = T + 2 * NB * 128

        def tern_gen(src_h, dst_h, n_k, n_m, m_t, s_t, tag_h, base, kmajor,
                     post=None, gp_every=0):
            nelem = float(n_k * 128 * n_m * 128)
            P = (n_m * 128) // 512
            pieces = [(k, p) for k in range(n_k) for p in range(P)]
            n = len(pieces)
            for i in range(n + 2):
                if i < n:
                    k, p = pieces[i]
                    sl, sres = tern_slabs[i % 4]
                    dma("sy", sl, src_h[k * 128:(k + 1) * 128,
                                        p * 512:(p + 1) * 512], wr=[sres])
                    yield
                if i >= 2:
                    k, p = pieces[i - 2]
                    sl, sres = tern_slabs[(i - 2) % 4]
                    E.emit("ve", lambda e, sl=sl, c=base + i - 2:
                           e.reduce_sum(accw[:, c:c + 1], sl, AX,
                                        apply_absolute_value=True),
                           [sres], [rr(accw)])
                yield
            E.emit("ve", lambda e: e.reduce_sum(accp[:, 0:1],
                                                accw[:, base:base + n], AX),
                   [rr(accw)], [rr(accp)])
            E.emit("te", lambda e: e.matmul(pb[7][0:1, 0:1], onest[:, 0:1],
                                            accp[:, 0:1], start=True, stop=True),
                   [rr(onest), rr(accp)], [rr(pb[7])])
            ts(tmn1[0:1, 0:1], pb[7][0:1, 0:1], 1.0 / nelem, 1e-5,
               ALU.mult, ALU.max, rd=[rr(pb[7])], wr=[rr(tmn1)])
            vcp(zrow[0:1, ZMCOL:ZMCOL + 1], tmn1[0:1, 0:1],
                rd=[rr(tmn1)], wr=[rr(zrow, 2)])
            E.emit("te", lambda e: e.matmul(
                pb[7][:, 0:1], onest[:, :], zrow[:, ZMCOL:ZMCOL + 1],
                start=True, stop=True),
                   [rr(onest), rr(zrow, 2)], [rr(pb[7])])
            vcp(m_t[:, :], pb[7][:, 0:1], rd=[rr(pb[7])], wr=[rr(m_t)])
            ts(thrp_t[:, :], m_t[:, :], 0.5, None, ALU.mult,
               rd=[rr(m_t)], wr=[rr(thrp_t)])
            ts(thrn_t[:, :], m_t[:, :], -0.5, None, ALU.mult,
               rd=[rr(m_t)], wr=[rr(thrn_t)])
            if post is not None:
                post()
            yield
            if not kmajor:
                pieces = [(k, p) for p in range(P) for k in range(n_k)]
            for i in range(n + 2):
                if i < n:
                    k, p = pieces[i]
                    sl, sres = tern_slabs[i % 4]
                    dma("sy", sl, src_h[k * 128:(k + 1) * 128,
                                        p * 512:(p + 1) * 512], wr=[sres])
                    yield
                if i >= 2:
                    k, p = pieces[i - 2]
                    sl, sres = tern_slabs[(i - 2) % 4]
                    obf = sbC if (i - 2) % 2 == 0 else sbD
                    ofres = rr(obf)
                    on_gp = gp_every > 0 and (i - 2) % gp_every == gp_every - 1
                    def _p2(e, sl=sl, obf=obf):
                        return e.tensor_scalar(obf[:, :], sl, thrp_t[:, :],
                                               -1.0, ALU.is_gt, ALU.add)
                    def _p2b(e, sl=sl):
                        return e.tensor_scalar(sl, sl, thrn_t[:, :], None,
                                               ALU.is_ge)
                    def _p2c(e, sl=sl, obf=obf):
                        return e.tensor_tensor(obf[:, :], obf[:, :], sl,
                                               ALU.add)
                    engn = "gp" if on_gp else "ve"
                    E.emit(engn, _p2, [sres, rr(thrp_t)], [ofres])
                    E.emit(engn, _p2b, [sres, rr(thrn_t)], [sres])
                    E.emit(engn, _p2c, [ofres, sres], [ofres])
                    if kmajor:
                        dma("sy", dst_h[k, :, p * 512:(p + 1) * 512],
                            obf[:, :], rd=[ofres], wr=[rr(tag_h, k)])
                    else:
                        m0 = p * 4
                        dma("sy",
                            dst_h[m0:m0 + 4, :, k * 128:(k + 1) * 128]
                            .rearrange("m p c -> p m c"),
                            obf[:, :].rearrange("p (m c) -> p m c", c=128),
                            rd=[ofres],
                            wr=[rr(tag_h, mm_) for mm_ in range(m0, m0 + 4)])
                yield

        def post_wg():
            ts(mg127_t[:, :], mg_t[:, :], 1.0 / 127.0, None, ALU.mult,
               rd=[rr(mg_t)], wr=[rr(mg127_t)])

        def post_wd():
            tt(mud_t[:, :], mu_t[:, :], md_t[:, :], ALU.mult,
               rd=[rr(mu_t), rr(md_t)], wr=[rr(mud_t)])
            ts(mud_t[:, :], mud_t[:, :], 1.0 / (127.0 * 127.0), None, ALU.mult,
               rd=[rr(mud_t)], wr=[rr(mud_t)])

        prep_done = set()

        def prep_chain():
            for name, g in (
                    ("wOm", cast_gen(wOmT_h, sOm_h, NCH, NCH, sOm_h)),
                    ("w1", cast_gen(w1T_h, sW1_h, NH1, NCTX, sW1_h)),
                    ("wg", tern_gen(wgT_h, sWg_h, NCH, NIN, mg_t, sg_t,
                                    sWg_h, 0, False, post_wg)),
                    ("w2", cast_gen(w2T_h, sW2_h, NCH, NH1, sW2_h)),
                    ("wu", tern_gen(wuT_h, sWu_h, NCH, NIN, mu_t, su_t,
                                    sWu_h, 0, False)),
                    ("wd", tern_gen(wdT_h, sWd_h, NIN, NCH, md_t, sd_t,
                                    sWd_h, 0, True, post_wd))):
                for _ in g:
                    yield
                prep_done.add(name)

        prep_iter = prep_chain()

        def drain(nsteps):
            for _ in range(nsteps):
                try:
                    next(prep_iter)
                except StopIteration:
                    return

        def ensure(name):
            while name not in prep_done:
                try:
                    next(prep_iter)
                except StopIteration:
                    prep_done.add(name)
                    return

        ws_state = {"big": 0, "small": 0}

        def load_big(scr_h, m, c0, ncols):
            idx = ws_state["big"]
            ws_state["big"] = (idx + 1) % 3
            buf = wb[idx]
            apb = buf[:, :ncols]
            res = rr(buf)
            dma("sy", apb, scr_h[m, :, c0:c0 + ncols],
                rd=[rr(scr_h, m)], wr=[res])
            return apb, res

        def load_small(scr_h, m, ncols):
            idx = ws_state["small"]
            ws_state["small"] = (idx + 1) % 5
            buf = wsm[idx]
            apb = buf[:, :ncols]
            res = rr(buf)
            dma("sy", apb, scr_h[m, :, :ncols], rd=[rr(scr_h, m)], wr=[res])
            return apb, res

        def ln_finalize(D_, banks=(4, 5)):
            b0, b1 = banks
            act(mu_b[:, :], pb[b0][:, :], AF.Identity, scale=1.0 / D_,
                rd=[rr(pb[b0])], wr=[rr(mu_b)])
            act(aux_b[:, :], pb[b1][:, :], AF.Identity, scale=1.0 / D_,
                rd=[rr(pb[b1])], wr=[rr(aux_b)])
            act(aux2_b[:, :], mu_b[:, :], AF.Square,
                rd=[rr(mu_b)], wr=[rr(aux2_b)])
            tt(aux_b[:, :], aux_b[:, :], aux2_b[:, :], ALU.subtract,
               rd=[rr(aux_b), rr(aux2_b)], wr=[rr(aux_b)])
            act(aux2_b[:, :], aux_b[:, :], AF.Ln, bias=cst[:, 0:1],
                rd=[rr(aux_b), rr(cst)], wr=[rr(aux2_b)])
            act(rstd_b[:, :], aux2_b[:, :], AF.Exp, scale=-0.5,
                rd=[rr(aux2_b)], wr=[rr(rstd_b)])
            tt(nmr_b[:, :], mu_b[:, :], rstd_b[:, :], ALU.mult,
               rd=[rr(mu_b), rr(rstd_b)], wr=[rr(nmr_b)])

        def ln_stat_step(i, nck, a, res, stage="glb", banks=(4, 5)):
            b0, b1 = banks
            buf, bres = (sbA, rr(sbA)) if i % 2 == 0 else (sbB, rr(sbB))
            if stage == "glb":
                c2 = 14 + (i % 2)
                buf2, bres2 = glb[:, c2 * T:(c2 + 1) * T], rr(glb, c2)
            else:
                c2 = i % 2
                buf2, bres2 = hnb[:, c2 * T:(c2 + 1) * T], rr(hnb, c2)
            act(buf[:, :], a, AF.Identity, rd=[res], wr=[bres])
            mm(pb[b0][:, :], onebf[:, :], buf[:, :], i == 0, i == nck - 1,
               rd=[rr(onebf), bres], wr=[rr(pb[b0])])
            act(buf2, a, AF.Square, rd=[res], wr=[bres2])
            mm(pb[b1][:, :], onebf[:, :], buf2, i == 0, i == nck - 1,
               rd=[rr(onebf), bres2], wr=[rr(pb[b1])])

        def round_to_bf16(v_ap, v_res, out_ap, out_res):
            act(qi32[:, :], v_ap, AF.Identity, bias=cst[:, 2:3],
                rd=[v_res, rr(cst)], wr=[rr(qi32)])
            stt(csC[:, :], qi32[:, :], -1.0, v_ap, ALU.mult, ALU.add,
                rd=[rr(qi32), v_res], wr=[rr(csC)])
            ts(csD[:, :], csC[:, :], 0.5, -1.0, ALU.is_gt, ALU.add,
               rd=[rr(csC)], wr=[rr(csD)])
            stt(csD[:, :], qi32[:, :], 1.0, csD[:, :], ALU.mult, ALU.add,
                rd=[rr(qi32), rr(csD)], wr=[rr(csD)])
            stt(out_ap, csC[:, :], -0.5, csD[:, :], ALU.is_ge, ALU.add,
                rd=[rr(csC), rr(csD)], wr=[out_res])

        def bcast_pair(n_used, out0, out0_res, out1, out1_res):
            E.emit("te", lambda e:
                   e.transpose(pb[7][0:2 * NB, 0:128], tmc[:, :], identt[:, :]),
                   [rr(tmc), rr(identt)], [rr(pb[7])])
            vcp(rowst[:, :], pb[7][0:2 * NB, 0:128],
                rd=[rr(pb[7])], wr=[rr(rowst)])
            dma("gp", zrow[0:1, T:T + 2 * NB * 128], rowst[:, :],
                rd=[rr(rowst)], wr=[rr(zrow, 1)])
            E.emit("te", lambda e:
                   e.matmul(pb[7][:, :], onest[:, :], zrow[:, T:2 * T],
                            start=True, stop=True),
                   [rr(onest), rr(zrow, 1)], [rr(pb[7])])
            vcp(out0, pb[7][:, :], rd=[rr(pb[7])], wr=[out0_res])
            if n_used > 1:
                E.emit("te", lambda e:
                       e.matmul(pb[7][:, :], onest[:, :], zrow[:, 2 * T:3 * T],
                                start=True, stop=True),
                       [rr(onest), rr(zrow, 1)], [rr(pb[7])])
                vcp(out1, pb[7][:, :], rd=[rr(pb[7])], wr=[out1_res])

        def rowmax_acc(i, a, res):
            sq, sqres = ((ivp_b, rr(ivp_b)) if i % 2 == 0
                         else (cfin_b, rr(cfin_b)))
            tt(sq[:, :], a, a, ALU.mult, rd=[res], wr=[sqres])
            if i == 0:
                vcp(posc_b[:, :], sq[:, :], rd=[sqres], wr=[rr(posc_b)])
            else:
                tt(posc_b[:, :], posc_b[:, :], sq[:, :], ALU.max,
                   rd=[rr(posc_b), sqres], wr=[rr(posc_b)])

        def rowmax_finish(out_tm_ap, out_tm_res):
            for b in range(NB):
                E.emit("te", lambda e, b=b:
                       e.transpose(pb[6][:, b * 128:(b + 1) * 128],
                                   posc_b[:, b * 128:(b + 1) * 128],
                                   identt[:, :]),
                       [rr(posc_b), rr(identt)], [rr(pb[6])])
            for b in range(NB):
                E.emit("ve", lambda e, b=b:
                       e.reduce_max(tmm[:, b:b + 1],
                                    pb[6][:, b * 128:(b + 1) * 128], AX),
                       [rr(pb[6])], [rr(tmm)])
            ts(out_tm_ap, tmm[:, :], 1e-10, None, ALU.max,
               rd=[rr(tmm)], wr=[out_tm_res])

        def emit_tile(t):
            seq_t = t % TPS
            first_in_seq = (seq_t == 0)
            posc_ap = posc_b[:, :]
            ivp_ap = ivp_b[:, :]
            E.emit("ve", lambda e, off=float(seq_t * T):
                   e.tensor_tensor_scan(zrow[0:1, 0:T], prow[0:1, :],
                                        prow[0:1, :], off, ALU.add, ALU.bypass),
                   [rr(prow)], [rr(zrow, 0)], no_inc=True)
            vcp(csD[0:1, 0:T], zrow[0:1, 0:T],
                rd=[rr(zrow, 0)], wr=[rr(csD), ("possh", t)])
            E.emit("te", lambda e:
                   e.matmul(pb[6][:, :], onest[:, :], zrow[:, 0:T],
                            start=True, stop=True),
                   [rr(onest), rr(zrow, 0), ("possh", t)], [rr(pb[6])])
            act(aux_b[:, :], pb[6][:, :], AF.Ln, rd=[rr(pb[6])], wr=[rr(aux_b)])
            act(posc_b[:, :], aux_b[:, :], AF.Exp, scale=-0.5,
                rd=[rr(aux_b)], wr=[rr(posc_b)])
            act(ivp_b[:, :], aux_b[:, :], AF.Exp, scale=-1.0,
                rd=[rr(aux_b)], wr=[rr(ivp_b)])
            drain(4)
            xbf = hnb
            xq = glb

            for hh in range(2):
                dma("sy", x_tm[:].rearrange("p (c d) -> p c d", d=D),
                    x_h[t * T + hh * 256:t * T + (hh + 1) * 256, :]
                    .rearrange("(c p) d -> p c d", p=128),
                    rd=rrs(x_tm, range(2)), wr=rrs(x_tm, range(2)))
                for f in range(NCH):
                    for tc in range(2):
                        E.emit("te", lambda e, f=f, tc=tc:
                               e.transpose(pb[6][:, tc * 128:(tc + 1) * 128],
                                           x_tm[:, tc * D + f * 128:tc * D + (f + 1) * 128],
                                           identt[:, :]),
                               rrs(x_tm, range(2)) + [rr(identt)],
                               [rr(pb[6])])
                    h0 = hh * 256
                    act(s1[:, f * T + h0:f * T + h0 + 256], pb[6][:, 0:256],
                        AF.Identity, rd=[rr(pb[6])], wr=[rr(s1, f)])
                    act(xbf[:, f * T + h0:f * T + h0 + 256],
                        s1[:, f * T + h0:f * T + h0 + 256],
                        AF.Identity, rd=[rr(s1, f)], wr=[rr(hnb, f)])
                    drain(1)

            ensure("wOm")
            for m in range(NCH):
                slab, sres = load_small(sOm_h, m, D)
                bank = m % 2
                for k in range(NCH):
                    mm(pb[bank][:, :], slab[:, k * 128:(k + 1) * 128],
                       xbf[:, k * T:(k + 1) * T], k == 0, k == NCH - 1,
                       rd=[sres, rr(hnb, k)], wr=[rr(pb[bank])])
                act(csA[:, :], pb[bank][:, :], AF.Identity,
                    bias=bels_t[:, m:m + 1], scale=els_t[:, m:m + 1],
                    rd=[rr(pb[bank]), rr(els_t), rr(bels_t)], wr=[rr(csA)])
                if m >= 1:
                    for j, slot in enumerate((s2, s3, s4, s5)):
                        ln_stat_step(4 * (m - 1) + j, NCTX,
                                     slot[:, (m - 1) * T:m * T],
                                     rr(slot, m - 1))
                tt(csA[:, :], csA[:, :], posc_ap, ALU.mult,
                   rd=[rr(csA), rr(posc_b)], wr=[rr(csA)])
                scan(csB[:, :], csA[:, :],
                     0.0 if first_in_seq else phc[:, m:m + 1],
                     rd=[rr(csA)] + ([] if first_in_seq else [rr(phc, m)]),
                     wr=[rr(csB)])
                ts(qi32[:, :], csB[:, :], INV_2PI, 0.5, ALU.mult, ALU.add,
                   rd=[rr(csB)], wr=[rr(qi32)])
                vcp(csC[:, :], qi32[:, :], rd=[rr(qi32)], wr=[rr(csC)])
                stt(csD[:, :], csC[:, :], -CW1, csB[:, :], ALU.mult, ALU.add,
                    rd=[rr(csC), rr(csB)], wr=[rr(csD)])
                vcp(phc[:, m:m + 1], csB[:, T - 1:T],
                    rd=[rr(csB)], wr=[rr(phc, m)])
                stt(csD[:, :], csC[:, :], -CW2, csD[:, :], ALU.mult, ALU.add,
                    rd=[rr(csC), rr(csD)], wr=[rr(csD)])
                ts(csC[:, :], csD[:, :], -PI_B, None, ALU.is_lt,
                   rd=[rr(csD)], wr=[rr(csC)])
                stt(csB[:, :], csC[:, :], CW1, csD[:, :], ALU.mult, ALU.add,
                    rd=[rr(csC), rr(csD)], wr=[rr(csB)])
                ts(csC[:, :], csB[:, :], PIH_HI, None, ALU.is_gt,
                   rd=[rr(csB)], wr=[rr(csC)])
                stt(csD[:, :], csC[:, :], -CW1, csB[:, :], ALU.mult, ALU.add,
                    rd=[rr(csC), rr(csB)], wr=[rr(csD)])
                ts(csD[:, :], csD[:, :], PIH_LO, PIH_HI, ALU.add, ALU.add,
                   rd=[rr(csD)], wr=[rr(csD)])
                ts(csB[:, :], csB[:, :], -PI_B, PI_B, ALU.max, ALU.min,
                   rd=[rr(csB)], wr=[rr(csB)])
                ts(csD[:, :], csD[:, :], -PI_B, PI_B, ALU.max, ALU.min,
                   rd=[rr(csD)], wr=[rr(csD)])
                act(sinc[:, :], csB[:, :], AF.Sin, rd=[rr(csB)], wr=[rr(sinc)])
                act(cosc[:, :], csD[:, :], AF.Sin, rd=[rr(csD)], wr=[rr(cosc)])
                xc = s1[:, m * T:(m + 1) * T]
                gtt(s2[:, m * T:(m + 1) * T], xc, cosc[:, :], ALU.mult,
                    rd=[rr(s1, m), rr(cosc)], wr=[rr(s2, m)])
                gtt(s3[:, m * T:(m + 1) * T], xc, sinc[:, :], ALU.mult,
                    rd=[rr(s1, m), rr(sinc)], wr=[rr(s3, m)])
                scan(csA[:, :], s2[:, m * T:(m + 1) * T],
                     0.0 if first_in_seq else mrc[:, m:m + 1],
                     rd=[rr(s2, m)] + ([] if first_in_seq else [rr(mrc, m)]),
                     wr=[rr(csA)])
                scan(csB[:, :], s3[:, m * T:(m + 1) * T],
                     0.0 if first_in_seq else mic[:, m:m + 1],
                     rd=[rr(s3, m)] + ([] if first_in_seq else [rr(mic, m)]),
                     wr=[rr(csB)])
                tt(csC[:, :], csA[:, :], ivp_ap, ALU.mult,
                   rd=[rr(csA), rr(ivp_b)], wr=[rr(csC)])
                tt(csD[:, :], csB[:, :], ivp_ap, ALU.mult,
                   rd=[rr(csB), rr(ivp_b)], wr=[rr(csD)])
                vcp(mrc[:, m:m + 1], csA[:, T - 1:T],
                    rd=[rr(csA)], wr=[rr(mrc, m)])
                vcp(mic[:, m:m + 1], csB[:, T - 1:T],
                    rd=[rr(csB)], wr=[rr(mic, m)])
                tt(csA[:, :], csC[:, :], cosc[:, :], ALU.mult,
                   rd=[rr(csC), rr(cosc)], wr=[rr(csA)])
                tt(csB[:, :], csD[:, :], sinc[:, :], ALU.mult,
                   rd=[rr(csD), rr(sinc)], wr=[rr(csB)])
                tt(s4[:, m * T:(m + 1) * T], csA[:, :], csB[:, :], ALU.add,
                   rd=[rr(csA), rr(csB)], wr=[rr(s4, m)])
                E.emit("gp", lambda e: e.tensor_tensor(
                    cosc[:, :], csD[:, :], cosc[:, :], ALU.mult),
                       [rr(csD), rr(cosc)], [rr(cosc)])
                E.emit("gp", lambda e: e.tensor_tensor(
                    sinc[:, :], csC[:, :], sinc[:, :], ALU.mult),
                       [rr(csC), rr(sinc)], [rr(sinc)])
                E.emit("gp", lambda e, m=m: e.tensor_tensor(
                    s5[:, m * T:(m + 1) * T], cosc[:, :], sinc[:, :],
                    ALU.subtract),
                       [rr(cosc), rr(sinc)], [rr(s5, m)])
                drain(4)

            for nm, slot in (("cm_r", s2), ("cm_i", s3), ("ret_r", s4),
                             ("ret_i", s5)):
                if nm in dbg_hs:
                    dma("sy", dbg_hs[nm][t, :, :], slot[:, :],
                        rd=rrs(slot, range(NCH)))

            def ctx_ap(cc):
                slot = (s2, s3, s4, s5)[cc // 8]
                m = cc % 8
                return slot[:, m * T:(m + 1) * T], rr(slot, m)

            for j, slot in enumerate((s2, s3, s4, s5)):
                ln_stat_step(4 * 7 + j, NCTX, slot[:, 7 * T:8 * T],
                             rr(slot, 7))
            ln_finalize(float(CTX))
            for cc in range(NCTX):
                a, res = ctx_ap(cc)
                buf = csA if cc % 2 == 0 else csB
                gtt(buf[:, :], a, rstd_b[:, :], ALU.mult,
                    rd=[res, rr(rstd_b)], wr=[rr(buf)])
                gtt(buf[:, :], buf[:, :], nmr_b[:, :], ALU.subtract,
                    rd=[rr(buf), rr(nmr_b)], wr=[rr(buf)])
                act(hnb[:, cc * T:(cc + 1) * T], buf[:, :], AF.Identity,
                    scale=lng_t[:, cc:cc + 1], bias=lnb_t[:, cc:cc + 1],
                    rd=[rr(buf), rr(lng_t), rr(lnb_t)], wr=[rr(hnb, cc)])
                drain(2)

            ensure("w1")
            for m in range(NH1):
                bank = m % 2
                for half in range(2):
                    slab, sres = load_big(sW1_h, m, half * 2048, 2048)
                    for kk in range(16):
                        k = half * 16 + kk
                        mm(pb[bank][:, :], slab[:, kk * 128:(kk + 1) * 128],
                           hnb[:, k * T:(k + 1) * T], k == 0, k == NCTX - 1,
                           rd=[sres, rr(hnb, k)], wr=[rr(pb[bank])])
                    drain(6)
                act(glb[:, m * T:(m + 1) * T], pb[bank][:, :], AF.Gelu,
                    bias=b1_t[:, m:m + 1],
                    rd=[rr(pb[bank]), rr(b1_t)], wr=[rr(glb, m)])

            ensure("w2")
            for m in range(NCH):
                slab, sres = load_big(sW2_h, m, 0, H1)
                bank = m % 2
                for k in range(NH1):
                    mm(pb[bank][:, :], slab[:, k * 128:(k + 1) * 128],
                       glb[:, k * T:(k + 1) * T], k == 0, k == NH1 - 1,
                       rd=[sres, rr(glb, k)], wr=[rr(pb[bank])])
                act(csA[:, :], pb[bank][:, :], AF.Identity, bias=b2_t[:, m:m + 1],
                    rd=[rr(pb[bank]), rr(b2_t)], wr=[rr(csA)])
                stt(s2[:, m * T:(m + 1) * T], s1[:, m * T:(m + 1) * T],
                    2.0, csA[:, :], ALU.mult, ALU.add,
                    rd=[rr(s1, m), rr(csA)], wr=[rr(s2, m)])
                ln_stat_step(m, NCH, s2[:, m * T:(m + 1) * T], rr(s2, m),
                             stage="hnb")
                drain(8)

            if "y1" in dbg_hs:
                dma("sy", dbg_hs["y1"][t, :, :], s2[:, :], rd=rrs(s2, range(NCH)))

            ln_finalize(float(D))
            for m in range(NCH):
                buf = csA if m % 2 == 0 else csB
                gtt(buf[:, :], s2[:, m * T:(m + 1) * T], rstd_b[:, :], ALU.mult,
                    rd=[rr(s2, m), rr(rstd_b)], wr=[rr(buf)])
                gtt(s3[:, m * T:(m + 1) * T], buf[:, :], nmr_b[:, :],
                    ALU.subtract, rd=[rr(buf), rr(nmr_b)], wr=[rr(s3, m)])
                rowmax_acc(m, s3[:, m * T:(m + 1) * T], rr(s3, m))

            if "x2" in dbg_hs:
                dma("sy", dbg_hs["x2"][t, :, :], s3[:, :], rd=rrs(s3, range(NCH)))

            ensure("wd")
            rowmax_finish(tms[:, :], rr(tms))
            act(tmm[:, 0:NB], tms[:, :], AF.Ln, rd=[rr(tms)], wr=[rr(tmm)])
            act(tmc[:, 0:NB], tmm[:, 0:NB], AF.Exp, scale=-0.5, bias=cst[:, 1:2],
                rd=[rr(tmm), rr(cst)], wr=[rr(tmc)])
            act(tmm[:, 0:NB], tms[:, :], AF.Sqrt, rd=[rr(tms)], wr=[rr(tmm)])
            ts(tmc[:, NB:2 * NB], tmm[:, 0:NB], mg127_t[:, 0:1], None, ALU.mult,
               rd=[rr(tmm), rr(mg127_t)], wr=[rr(tmc)])
            bcast_pair(2, s1q_b[:, :], rr(s1q_b), cg_b[:, :], rr(cg_b))
            for m in range(NCH):
                gtt(csB[:, :], s3[:, m * T:(m + 1) * T], s1q_b[:, :],
                    ALU.mult, rd=[rr(s3, m), rr(s1q_b)], wr=[rr(csB)])
                round_to_bf16(csB[:, :], rr(csB),
                              xq[:, m * T:(m + 1) * T], rr(glb, m))
                drain(2)

            def hf_ap(o):
                slot = (s1, s2, s4, s5)[o // 8]
                m = o % 8
                return slot[:, m * T:(m + 1) * T], rr(slot, m)

            ensure("wu")
            for o in range(NIN):
                gslab, gres = load_small(sWg_h, o, D)
                uslab, ures = load_small(sWu_h, o, D)
                gbank, ubank = (0, 1) if o % 2 == 0 else (2, 3)
                gst = gps0 if o % 2 == 0 else gps1
                ust = ups0 if o % 2 == 0 else ups1
                for k in range(NCH):
                    mm(pb[gbank][:, :], gslab[:, k * 128:(k + 1) * 128],
                       xq[:, k * T:(k + 1) * T], k == 0, k == NCH - 1,
                       rd=[gres, rr(glb, k)], wr=[rr(pb[gbank])])
                for k in range(NCH):
                    mm(pb[ubank][:, :], uslab[:, k * 128:(k + 1) * 128],
                       xq[:, k * T:(k + 1) * T], k == 0, k == NCH - 1,
                       rd=[ures, rr(glb, k)], wr=[rr(pb[ubank])])
                act(gst[:, :], pb[gbank][:, :], AF.Identity,
                    rd=[rr(pb[gbank])], wr=[rr(gst)])
                act(ust[:, :], pb[ubank][:, :], AF.Identity,
                    rd=[rr(pb[ubank])], wr=[rr(ust)])
                tt(csA[:, :], gst[:, :], cg_b[:, :], ALU.mult,
                   rd=[rr(gst), rr(cg_b)], wr=[rr(csA)])
                act(csB[:, :], csA[:, :], AF.Silu, rd=[rr(csA)], wr=[rr(csB)])
                ha, hres = hf_ap(o)
                E.emit("gp", lambda e, ha=ha, u=ust:
                       e.tensor_tensor(ha, csB[:, :], u[:, :], ALU.mult),
                       [rr(csB), rr(ust)], [hres])
                rowmax_acc(o, ha, hres)
                drain(4)

            rowmax_finish(tml[:, 0:NB], rr(tml))
            act(tmm[:, 0:NB], tml[:, 0:NB], AF.Ln, rd=[rr(tml)], wr=[rr(tmm)])
            act(tmc[:, 0:NB], tmm[:, 0:NB], AF.Exp, scale=-0.5, bias=cst[:, 1:2],
                rd=[rr(tmm), rr(cst)], wr=[rr(tmc)])
            act(tmq[:, 0:NB], tml[:, 0:NB], AF.Sqrt, rd=[rr(tml)], wr=[rr(tmq)])
            act(tmm[:, 0:NB], tms[:, :], AF.Sqrt, rd=[rr(tms)], wr=[rr(tmm)])
            tt(tmc[:, NB:2 * NB], tmq[:, 0:NB], tmm[:, 0:NB], ALU.mult,
               rd=[rr(tmq), rr(tmm)], wr=[rr(tmc)])
            ts(tmc[:, NB:2 * NB], tmc[:, NB:2 * NB], mud_t[:, 0:1], None, ALU.mult,
               rd=[rr(tmc), rr(mud_t)], wr=[rr(tmc)])
            bcast_pair(2, s2q_b[:, :], rr(s2q_b), cfin_b[:, :], rr(cfin_b))
            for o in range(NIN):
                ha, hres = hf_ap(o)
                gtt(csB[:, :], ha, s2q_b[:, :], ALU.mult,
                    rd=[hres, rr(s2q_b)], wr=[rr(csB)])
                round_to_bf16(csB[:, :], rr(csB),
                              hnb[:, o * T:(o + 1) * T], rr(hnb, o))
                drain(1)

            if "scales" in dbg_hs:
                for i, (bt, res) in enumerate([(s1q_b, rr(s1q_b)), (cg_b, rr(cg_b)),
                                               (s2q_b, rr(s2q_b)), (cfin_b, rr(cfin_b))]):
                    dma("sy", dbg_hs["scales"][t, :, i * T:(i + 1) * T], bt[:, :],
                        rd=[res])
            if "hf0" in dbg_hs:
                for i, slot in enumerate((s1, s2, s4, s5)):
                    dma("sy", dbg_hs["hf%d" % i][t, :, :], slot[:, :],
                        rd=rrs(slot, range(NCH)))

            for k in range(NIN):
                slab, sres = load_small(sWd_h, k, D)
                for m in range(NCH):
                    mm(pb[m][:, :], slab[:, m * 128:(m + 1) * 128],
                       hnb[:, k * T:(k + 1) * T], k == 0, k == NIN - 1,
                       rd=[sres, rr(hnb, k)], wr=[rr(pb[m])])
            evord = [6, 7, 0, 1, 2, 3, 4, 5]
            for p, m in enumerate(evord):
                st = gps0 if p % 2 == 0 else gps1
                act(st[:, :], pb[m][:, :], AF.Identity,
                    rd=[rr(pb[m])], wr=[rr(st)])
                gtt(csA[:, :], st[:, :], cfin_b[:, :], ALU.mult,
                    rd=[rr(st), rr(cfin_b)], wr=[rr(csA)])
                tt(s4[:, m * T:(m + 1) * T], csA[:, :],
                   s3[:, m * T:(m + 1) * T], ALU.add,
                   rd=[rr(csA), rr(s3, m)], wr=[rr(s4, m)])
                if p >= 2:
                    ms = evord[p - 2]
                    ln_stat_step(p - 2, NCH, s4[:, ms * T:(ms + 1) * T],
                                 rr(s4, ms), stage="hnb", banks=(6, 7))
                drain(2)
            for p in (6, 7):
                ms = evord[p]
                ln_stat_step(p, NCH, s4[:, ms * T:(ms + 1) * T],
                             rr(s4, ms), stage="hnb", banks=(6, 7))

            if "y2" in dbg_hs:
                dma("sy", dbg_hs["y2"][t, :, :], s4[:, :], rd=rrs(s4, range(NCH)))

            ln_finalize(float(D), banks=(6, 7))
            for m in range(NCH):
                buf = csA if m % 2 == 0 else csB
                gtt(buf[:, :], s4[:, m * T:(m + 1) * T], rstd_b[:, :], ALU.mult,
                    rd=[rr(s4, m), rr(rstd_b)], wr=[rr(buf)])
                gtt(s5[:, m * T:(m + 1) * T], buf[:, :], nmr_b[:, :],
                    ALU.subtract, rd=[rr(buf), rr(nmr_b)], wr=[rr(s5, m)])
                drain(2)

            for hh in range(2):
                for tc in range(2):
                    for f in range(NCH):
                        bank = 6 if f < 4 else 7
                        E.emit("te", lambda e, tc=tc, f=f, bank=bank, hh=hh:
                               e.transpose(pb[bank][:, (f % 4) * 128:(f % 4 + 1) * 128],
                                           s5[:, f * T + (hh * 2 + tc) * 128:
                                              f * T + (hh * 2 + tc + 1) * 128],
                                           identt[:, :]),
                               [rr(s5, f), rr(identt)], [rr(pb[bank])])
                    act(x_tm[:, tc * D:tc * D + 512], pb[6][:, :],
                        AF.Identity, rd=[rr(pb[6])], wr=[rr(x_tm, tc)])
                    act(x_tm[:, tc * D + 512:(tc + 1) * D], pb[7][:, :],
                        AF.Identity, rd=[rr(pb[7])], wr=[rr(x_tm, tc)])
                dma("sy",
                    out_h[t * T + hh * 256:t * T + (hh + 1) * 256, :]
                    .rearrange("(c p) d -> p c d", p=128),
                    x_tm[:].rearrange("p (c d) -> p c d", d=D),
                    rd=rrs(x_tm, range(2)), wr=[("out", t, hh)])

        for t in range(NT):
            emit_tile(t)

        E.build(block, sems, dma_sems)

    return nc


_prog_cache = {}


def _get_prog():
    key = tuple(_DBG_WANT)
    if key not in _prog_cache:
        _prog_cache[key] = build_program()
    return _prog_cache[key]


def _pp(vec, nch):
    return np.ascontiguousarray(np.asarray(vec, np.float32).reshape(nch, 128).T)


def kernel(L0_attn_x, L0_attn_W_omega, L0_attn_b_omega, L0_attn_log_scale,
           L0_attn_ln_gamma, L0_attn_ln_beta, L0_attn_W1, L0_attn_b1,
           L0_attn_W2, L0_attn_b2, L0_ffn_w_gate, L0_ffn_w_up, L0_ffn_w_down):
    x = np.asarray(L0_attn_x, np.float32)
    shared = dict(
        wOmT=np.ascontiguousarray(np.asarray(L0_attn_W_omega, np.float32).T),
        w1T=np.ascontiguousarray(np.asarray(L0_attn_W1, np.float32).T),
        w2T=np.ascontiguousarray(np.asarray(L0_attn_W2, np.float32).T),
        wgT=np.ascontiguousarray(np.asarray(L0_ffn_w_gate, np.float32).T),
        wuT=np.ascontiguousarray(np.asarray(L0_ffn_w_up, np.float32).T),
        wdT=np.ascontiguousarray(np.asarray(L0_ffn_w_down, np.float32).T),
        bom=_pp(L0_attn_b_omega, NCH),
        lsc=_pp(L0_attn_log_scale, NCH),
        lng=_pp(L0_attn_ln_gamma, NCTX),
        lnb=_pp(L0_attn_ln_beta, NCTX),
        b1v=_pp(L0_attn_b1, NH1),
        b2v=_pp(L0_attn_b2, NCH),
        ident=np.eye(128, dtype=np.float32),
        ones32=np.ones((128, 128), np.float32),
    )
    in_maps = []
    for c in range(NCORE):
        m = dict(shared)
        m["x"] = np.ascontiguousarray(
            x[c * SEQ_PER_CORE:(c + 1) * SEQ_PER_CORE].reshape(TOK, D))
        in_maps.append(m)
    nc = _get_prog()
    res = run_bass_kernel_spmd(nc, in_maps, core_ids=list(range(NCORE)))
    out = np.empty((B, S, D), np.float32)
    for c in range(NCORE):
        out[c * SEQ_PER_CORE:(c + 1) * SEQ_PER_CORE] = \
            np.asarray(res.results[c]["out"], np.float32).reshape(
                SEQ_PER_CORE, S, D)
    return out


# version 23 (final)
# speedup vs baseline: 1.1149x; 1.0417x over previous
import math
import numpy as np
import concourse.bass as bass
import concourse.mybir as mybir
from concourse.bass_utils import run_bass_kernel_spmd

F32 = mybir.dt.float32
I32 = mybir.dt.int32
BF16 = mybir.dt.bfloat16
AF = mybir.ActivationFunctionType
ALU = mybir.AluOpType
AX = mybir.AxisListType.X

B, S, D = 16, 2048, 1024
NCORE = 8
SEQ_PER_CORE = B // NCORE
TOK = SEQ_PER_CORE * S
T = 512
NT = TOK // T
TPS = S // T
NCH = D // 128
CTX = 4 * D
H1 = 2 * D
INNER = 4 * D
NCTX = CTX // 128
NH1 = H1 // 128
NIN = INNER // 128
NB = T // 128
LN_EPS = 1e-5

TWO_PI = 2.0 * math.pi
INV_2PI = float(np.float32(1.0 / TWO_PI))
CW1 = float(np.float32(TWO_PI))
CW2 = float(np.float32(TWO_PI - CW1))
PI_B = float(np.float32(np.nextafter(np.float32(math.pi), np.float32(0.0))))
PIH_HI = float(np.float32(math.pi / 2))
PIH_LO = float(np.float32(math.pi / 2 - PIH_HI))
LN127 = float(np.float32(math.log(127.0)))

_DBG_WANT = []
_TRUNC = None


class Emitter:
    ENGINES = ("te", "ve", "ac", "sy", "gp")
    NDS = 8

    def __init__(self):
        self.steps = []

    def emit(self, eng, fn, reads=(), writes=(), dma=False, async_p=False,
             no_inc=False):
        self.steps.append((eng, fn, tuple(reads), tuple(writes), dma, async_p,
                           no_inc))

    def dma(self, queue, fn, reads=(), writes=()):
        self.emit(queue, fn, reads, writes, dma=True)

    def build(self, block, sems, dma_sems):
        if _TRUNC is not None:
            self.steps = self.steps[:_TRUNC]
        steps = self.steps
        n = len(steps)
        NDS = self.NDS
        last_write = {}
        readers = {}
        deps = [set() for _ in range(n)]
        pre_dcnt = {"sy": 0, "ac": 0, "gp": 0}
        war_key = [None] * n
        for i, (eng, fn, reads, writes, is_dma, _, _) in enumerate(steps):
            if is_dma:
                war_key[i] = (eng, "dma", pre_dcnt[eng] % NDS)
                pre_dcnt[eng] += 1
            else:
                war_key[i] = (eng,)
            for r in reads:
                if r in last_write:
                    deps[i].add(last_write[r])
            for w in writes:
                if w in last_write:
                    deps[i].add(last_write[w])
                by_stream = {}
                for rd in readers.get(w, ()):
                    by_stream[war_key[rd]] = rd
                deps[i].update(by_stream.values())
            for r in reads:
                readers.setdefault(r, []).append(i)
            for w in writes:
                last_write[w] = i
                readers[w] = []
        flagged = [False] * n
        for i, dset in enumerate(deps):
            for j in dset:
                if steps[j][4]:
                    continue
                if steps[j][6]:
                    continue
                if (steps[j][0] != steps[i][0] or steps[j][5]
                        or steps[j][0] in ("ac", "ve", "gp")):
                    flagged[j] = True
        NDS = self.NDS
        seqno = [0] * n
        dslot = [0] * n
        cnt = {e: 0 for e in self.ENGINES}
        dcnt = {"sy": 0, "ac": 0, "gp": 0}
        slot_cnt = {}
        for i, st in enumerate(steps):
            if st[4]:
                s = dcnt[st[0]] % NDS
                dcnt[st[0]] += 1
                dslot[i] = s
                slot_cnt[(st[0], s)] = slot_cnt.get((st[0], s), 0) + 1
                seqno[i] = slot_cnt[(st[0], s)]
            elif flagged[i]:
                cnt[st[0]] += 1
                seqno[i] = cnt[st[0]]
        per_eng = {e: [] for e in self.ENGINES}
        for i, st in enumerate(steps):
            per_eng[st[0]].append(i)

        def run_engine(eng_name, eng):
            waited = {}

            def wait(key, sem, val):
                if waited.get(key, 0) >= val:
                    return
                eng.wait_ge(sem, val)
                waited[key] = val

            for i in per_eng[eng_name]:
                _, fn, reads, writes, is_dma, _asy, _ni = steps[i]
                need = {}
                for j in deps[i]:
                    src = steps[j][0]
                    if steps[j][4]:
                        key = (src, "dma", dslot[j])
                        val = seqno[j] * 16
                        sem = dma_sems[src][dslot[j]]
                    else:
                        if src == eng_name and not flagged[j]:
                            continue
                        key, val, sem = (src, "sem"), seqno[j], sems[src]
                    if val > need.get(key, (0, None))[0]:
                        need[key] = (val, sem)
                for key, (val, sem) in sorted(need.items(), key=str):
                    wait(key, sem, val)
                if is_dma:
                    s = dslot[i]
                    if seqno[i] > 1:
                        wait((eng_name, "dma", s), dma_sems[eng_name][s],
                             (seqno[i] - 1) * 16)
                inst = fn(eng)
                if is_dma:
                    inst.then_inc(dma_sems[eng_name][dslot[i]], 16)
                elif flagged[i]:
                    inst.then_inc(sems[eng_name], 1)
            if eng_name == "sy":
                for q in ("sy", "ac", "gp"):
                    for s in range(NDS):
                        tot = slot_cnt.get((q, s), 0)
                        if tot > 0:
                            eng.wait_ge(dma_sems[q][s], tot * 16)

        @block.tensor
        def _(tensor):
            run_engine("te", tensor)

        @block.vector
        def _(vector):
            run_engine("ve", vector)

        @block.scalar
        def _(scalar):
            run_engine("ac", scalar)

        @block.sync
        def _(sync):
            run_engine("sy", sync)

        @block.gpsimd
        def _(gpsimd):
            run_engine("gp", gpsimd)


def build_program():
    nc = bass.Bass("TRN2", target_bir_lowering=False,
                   detect_race_conditions=False)

    x_h = nc.dram_tensor("x", [TOK, D], F32, kind="ExternalInput").ap()
    wOmT_h = nc.dram_tensor("wOmT", [D, D], F32, kind="ExternalInput").ap()
    w1T_h = nc.dram_tensor("w1T", [CTX, H1], F32, kind="ExternalInput").ap()
    w2T_h = nc.dram_tensor("w2T", [H1, D], F32, kind="ExternalInput").ap()
    wgT_h = nc.dram_tensor("wgT", [D, INNER], F32, kind="ExternalInput").ap()
    wuT_h = nc.dram_tensor("wuT", [D, INNER], F32, kind="ExternalInput").ap()
    wdT_h = nc.dram_tensor("wdT", [INNER, D], F32, kind="ExternalInput").ap()
    bom_h = nc.dram_tensor("bom", [128, NCH], F32, kind="ExternalInput").ap()
    lsc_h = nc.dram_tensor("lsc", [128, NCH], F32, kind="ExternalInput").ap()
    lng_h = nc.dram_tensor("lng", [128, NCTX], F32, kind="ExternalInput").ap()
    lnb_h = nc.dram_tensor("lnb", [128, NCTX], F32, kind="ExternalInput").ap()
    b1_h = nc.dram_tensor("b1v", [128, NH1], F32, kind="ExternalInput").ap()
    b2_h = nc.dram_tensor("b2v", [128, NCH], F32, kind="ExternalInput").ap()
    ident_h = nc.dram_tensor("ident", [128, 128], F32, kind="ExternalInput").ap()
    ones32_h = nc.dram_tensor("ones32", [128, 128], F32, kind="ExternalInput").ap()
    out_h = nc.dram_tensor("out", [TOK, D], F32, kind="ExternalOutput").ap()

    sOm_h = nc.dram_tensor("sOm", [NCH, 128, D], BF16, kind="Internal").ap()
    sW1_h = nc.dram_tensor("sW1", [NH1, 128, CTX], BF16, kind="Internal").ap()
    sW2_h = nc.dram_tensor("sW2", [NCH, 128, H1], BF16, kind="Internal").ap()
    sWg_h = nc.dram_tensor("sWg", [NIN, 128, D], BF16, kind="Internal").ap()
    sWu_h = nc.dram_tensor("sWu", [NIN, 128, D], BF16, kind="Internal").ap()
    sWd_h = nc.dram_tensor("sWd", [NIN, 128, D], BF16, kind="Internal").ap()

    dbg_hs = {}
    for name in _DBG_WANT:
        dbg_hs[name] = nc.dram_tensor("dbg_" + name, [NT, 128, NCH * T], F32,
                                      kind="ExternalOutput").ap()

    from contextlib import ExitStack
    es = ExitStack()
    with es:
        def sbt(name, shape, dt=F32):
            return es.enter_context(nc.sbuf_tensor(name, shape, dt))

        s1 = sbt("s1", [128, NCH * T]); s2 = sbt("s2", [128, NCH * T])
        s3 = sbt("s3", [128, NCH * T]); s4 = sbt("s4", [128, NCH * T])
        s5 = sbt("s5", [128, NCH * T])
        x_tm = sbt("x_tm", [128, 2 * D])
        hnb = sbt("hnb", [128, NCTX * T], BF16)
        glb = sbt("glb", [128, NH1 * T], BF16)
        wb = [sbt(f"wb{i}", [128, 2048], BF16) for i in range(3)]
        wsm = [sbt(f"wsm{i}", [128, 1024], BF16) for i in range(5)]
        csA = sbt("csA", [128, T]); csB = sbt("csB", [128, T])

        csC = sbt("csC", [128, T]); csD = sbt("csD", [128, T])
        sinc = sbt("sinc", [128, T]); cosc = sbt("cosc", [128, T])
        qi32 = sbt("qi32", [128, T], I32)
        sbA = sbt("sbA", [128, T], BF16); sbB = sbt("sbB", [128, T], BF16)
        mu_b = sbt("mu_b", [128, T]); rstd_b = sbt("rstd_b", [128, T])
        gps0 = mu_b; gps1 = rstd_b
        nmr_b = sbt("nmr_b", [128, T]); aux_b = sbt("aux_b", [128, T])
        ups0 = nmr_b; ups1 = aux_b
        aux2_b = csD
        s1q_b = sbt("s1q_b", [128, T]); cg_b = sbt("cg_b", [128, T])
        s2q_b = sbt("s2q_b", [128, T]); cfin_b = sbt("cfin_b", [128, T])
        posc_b = sbt("posc_b", [128, T]); ivp_b = sbt("ivp_b", [128, T])
        prow = sbt("prow", [1, T])
        zrow = sbt("zrow", [128, T + 2 * NB * 128 + 4])
        tmm = sbt("tmm", [128, NB]); tms = sbt("tms", [128, NB])
        tmq = sbt("tmq", [128, NB])
        tml = sbt("tml", [128, NB])
        tmc = sbt("tmc", [128, 2 * NB])
        rowst = sbt("rowst", [2 * NB, 128])
        identt = sbt("identt", [128, 128]); onest = sbt("onest", [128, 128])
        onebf = sbt("onebf", [128, 128], BF16)
        bom_t = sbt("bom_t", [128, NCH]); lsc_t = sbt("lsc_t", [128, NCH])
        els_t = sbt("els_t", [128, NCH]); bels_t = sbt("bels_t", [128, NCH])
        lng_t = sbt("lng_t", [128, NCTX]); lnb_t = sbt("lnb_t", [128, NCTX])
        b1_t = sbt("b1_t", [128, NH1]); b2_t = sbt("b2_t", [128, NCH])
        mg_t = sbt("mg_t", [128, 1]); mu_t = sbt("mu_t", [128, 1])
        md_t = sbt("md_t", [128, 1])
        sg_t = sbt("sg_t", [128, 1]); su_t = sbt("su_t", [128, 1])
        sd_t = sbt("sd_t", [128, 1])
        mud_t = sbt("mud_t", [128, 1]); mg127_t = sbt("mg127_t", [128, 1])
        accp = sbt("accp", [128, 1]); accw = sbt("accw", [128, 64])
        thrp_t = sbt("thrp_t", [128, 1]); thrn_t = sbt("thrn_t", [128, 1])
        sbC = sbt("sbC", [128, T], BF16); sbD = sbt("sbD", [128, T], BF16)
        tmn1 = sbt("tmn1", [128, 1]); tmn2 = sbt("tmn2", [128, 1])
        cst = sbt("cst", [128, 3])
        phc = sbt("phc", [128, NCH]); mrc = sbt("mrc", [128, NCH])
        mic = sbt("mic", [128, NCH])

        pb = [es.enter_context(nc.psum_tensor(f"pb{i}", [128, T], F32))
              for i in range(8)]

        sems = {e: es.enter_context(nc.semaphore("sem_" + e))
                for e in Emitter.ENGINES}
        dma_sems = {q: [es.enter_context(nc.semaphore(f"dsem_{q}{i}"))
                        for i in range(Emitter.NDS)]
                    for q in ("sy", "ac", "gp")}
        block = es.enter_context(nc.Block())

        E = Emitter()

        def rr(h, c=0):
            return (id(h), c)

        def rrs(h, cs):
            return [(id(h), c) for c in cs]

        def ts(out, in0, sc1, sc2, op0, op1=None, rd=(), wr=()):
            if op1 is None:
                E.emit("ve", lambda e, o=out, i=in0, a=sc1, p0=op0:
                       e.tensor_scalar(o, i, a, None, p0), rd, wr)
            else:
                E.emit("ve", lambda e, o=out, i=in0, a=sc1, b=sc2, p0=op0, p1=op1:
                       e.tensor_scalar(o, i, a, b, p0, p1), rd, wr)

        def tt(out, in0, in1, op, rd=(), wr=()):
            E.emit("ve", lambda e, o=out, i=in0, j=in1, p=op:
                   e.tensor_tensor(o, i, j, p), rd, wr)

        def gtt(out, in0, in1, op, rd=(), wr=()):
            E.emit("gp", lambda e, o=out, i=in0, j=in1, p=op:
                   e.tensor_tensor(o, i, j, p), rd, wr)

        def stt(out, in0, sc, in1, op0, op1, rd=(), wr=()):
            E.emit("ve", lambda e, o=out, i=in0, s=sc, j=in1, p0=op0, p1=op1:
                   e.scalar_tensor_tensor(o, i, s, j, p0, p1), rd, wr)

        def act(out, in_, func, bias=0.0, scale=1.0, accum=None, rd=(), wr=()):
            E.emit("ac", lambda e, o=out, i=in_, f=func, b=bias, s=scale, a=accum:
                   e.activation(o, i, f, bias=b, scale=s, accum_out=a), rd, wr)

        def vcp(out, in_, rd=(), wr=()):
            E.emit("ve", lambda e, o=out, i=in_: e.tensor_copy(o, i), rd, wr)

        def mm(out, lhsT, rhs, start, stop, rd=(), wr=()):
            E.emit("te", lambda e, o=out, l=lhsT, r=rhs, a=start, b=stop:
                   e.matmul(o, l, r, start=a, stop=b), rd, wr)

        def tr(out, in_, rd=(), wr=()):
            E.emit("te", lambda e, o=out, i=in_:
                   e.transpose(o, i, identt[:, :]), rd, wr)

        def dma(q, out, in_, rd=(), wr=()):
            E.dma(q, lambda e, o=out, i=in_: e.dma_start(out=o, in_=i), rd, wr)

        def scan(out, data0, init, rd=(), wr=()):
            E.emit("ve", lambda e, o=out, d=data0, v=init:
                   e.tensor_tensor_scan(o, d, d, v, ALU.add, ALU.bypass),
                   rd, wr, no_inc=True)

        dma("sy", identt[:, :], ident_h, wr=[rr(identt)])
        dma("sy", onest[:, :], ones32_h, wr=[rr(onest)])
        act(onebf[:, :], onest[:, :], AF.Identity, rd=[rr(onest)], wr=[rr(onebf)])
        dma("sy", bom_t[:, :], bom_h, wr=[rr(bom_t)])
        dma("sy", lsc_t[:, :], lsc_h, wr=[rr(lsc_t)])
        dma("sy", lng_t[:, :], lng_h, wr=[rr(lng_t)])
        dma("sy", lnb_t[:, :], lnb_h, wr=[rr(lnb_t)])
        dma("sy", b1_t[:, :], b1_h, wr=[rr(b1_t)])
        dma("sy", b2_t[:, :], b2_h, wr=[rr(b2_t)])
        act(els_t[:, :], lsc_t[:, :], AF.Exp, rd=[rr(lsc_t)], wr=[rr(els_t)])
        tt(bels_t[:, :], bom_t[:, :], els_t[:, :], ALU.mult,
           rd=[rr(bom_t), rr(els_t)], wr=[rr(bels_t)])

        E.emit("ve", lambda e: e.memset(prow[0:1, :], 1.0), (), [rr(prow)], async_p=True)
        E.emit("ve", lambda e: e.memset(zrow[:, :], 0.0), (),
               [rr(zrow, c) for c in range(5)], async_p=True)
        E.emit("ve", lambda e: e.memset(cst[:, 0:1], LN_EPS), (), [rr(cst)], async_p=True)
        E.emit("ve", lambda e: e.memset(cst[:, 1:2], LN127), (), [rr(cst)], async_p=True)
        E.emit("ve", lambda e: e.memset(cst[:, 2:3], 0.5), (), [rr(cst)], async_p=True)

        def cast_gen(src_h, dst_h, n_m, n_k, tag_h):
            for m in range(n_m):
                for kh in range((n_k + 15) // 16):
                    k0, k1 = kh * 16, min(n_k, (kh + 1) * 16)
                    src = src_h[k0 * 128:k1 * 128, m * 128:(m + 1) * 128] \
                        .rearrange("(k p) c -> p k c", p=128)
                    dst = dst_h[m, :, k0 * 128:k1 * 128] \
                        .rearrange("p (k c) -> p k c", c=128)
                    dma("gp", dst, src, wr=[rr(tag_h, m)])
                    yield

        tern_slabs = [(s1q_b[:, :], rr(s1q_b)), (s2q_b[:, :], rr(s2q_b)),
                      (x_tm[:, 0:512], rr(x_tm, 0)),
                      (x_tm[:, 1024:1536], rr(x_tm, 1))]
        tern_outs = [(cg_b[:, :], rr(cg_b)), (cfin_b[:, :], rr(cfin_b))]
        ZMCOL = T + 2 * NB * 128

        def tern_gen(src_h, dst_h, n_k, n_m, m_t, s_t, tag_h, base, kmajor,
                     post=None, gp_every=0):
            nelem = float(n_k * 128 * n_m * 128)
            P = (n_m * 128) // 512
            pieces = [(k, p) for k in range(n_k) for p in range(P)]
            n = len(pieces)
            for i in range(n + 2):
                if i < n:
                    k, p = pieces[i]
                    sl, sres = tern_slabs[i % 4]
                    dma("sy", sl, src_h[k * 128:(k + 1) * 128,
                                        p * 512:(p + 1) * 512], wr=[sres])
                    yield
                if i >= 2:
                    k, p = pieces[i - 2]
                    sl, sres = tern_slabs[(i - 2) % 4]
                    E.emit("ve", lambda e, sl=sl, c=base + i - 2:
                           e.reduce_sum(accw[:, c:c + 1], sl, AX,
                                        apply_absolute_value=True),
                           [sres], [rr(accw)])
                yield
            E.emit("ve", lambda e: e.reduce_sum(accp[:, 0:1],
                                                accw[:, base:base + n], AX),
                   [rr(accw)], [rr(accp)])
            E.emit("te", lambda e: e.matmul(pb[7][0:1, 0:1], onest[:, 0:1],
                                            accp[:, 0:1], start=True, stop=True),
                   [rr(onest), rr(accp)], [rr(pb[7])])
            ts(tmn1[0:1, 0:1], pb[7][0:1, 0:1], 1.0 / nelem, 1e-5,
               ALU.mult, ALU.max, rd=[rr(pb[7])], wr=[rr(tmn1)])
            vcp(zrow[0:1, ZMCOL:ZMCOL + 1], tmn1[0:1, 0:1],
                rd=[rr(tmn1)], wr=[rr(zrow, 2)])
            E.emit("te", lambda e: e.matmul(
                pb[7][:, 0:1], onest[:, :], zrow[:, ZMCOL:ZMCOL + 1],
                start=True, stop=True),
                   [rr(onest), rr(zrow, 2)], [rr(pb[7])])
            vcp(m_t[:, :], pb[7][:, 0:1], rd=[rr(pb[7])], wr=[rr(m_t)])
            ts(thrp_t[:, :], m_t[:, :], 0.5, None, ALU.mult,
               rd=[rr(m_t)], wr=[rr(thrp_t)])
            ts(thrn_t[:, :], m_t[:, :], -0.5, None, ALU.mult,
               rd=[rr(m_t)], wr=[rr(thrn_t)])
            if post is not None:
                post()
            yield
            if not kmajor:
                pieces = [(k, p) for p in range(P) for k in range(n_k)]
            for i in range(n + 2):
                if i < n:
                    k, p = pieces[i]
                    sl, sres = tern_slabs[i % 4]
                    dma("sy", sl, src_h[k * 128:(k + 1) * 128,
                                        p * 512:(p + 1) * 512], wr=[sres])
                    yield
                if i >= 2:
                    k, p = pieces[i - 2]
                    sl, sres = tern_slabs[(i - 2) % 4]
                    obf = sbC if (i - 2) % 2 == 0 else sbD
                    ofres = rr(obf)
                    on_gp = gp_every > 0 and (i - 2) % gp_every == gp_every - 1
                    def _p2(e, sl=sl, obf=obf):
                        return e.tensor_scalar(obf[:, :], sl, thrp_t[:, :],
                                               -1.0, ALU.is_gt, ALU.add)
                    def _p2b(e, sl=sl):
                        return e.tensor_scalar(sl, sl, thrn_t[:, :], None,
                                               ALU.is_ge)
                    def _p2c(e, sl=sl, obf=obf):
                        return e.tensor_tensor(obf[:, :], obf[:, :], sl,
                                               ALU.add)
                    engn = "gp" if on_gp else "ve"
                    E.emit(engn, _p2, [sres, rr(thrp_t)], [ofres])
                    E.emit(engn, _p2b, [sres, rr(thrn_t)], [sres])
                    E.emit(engn, _p2c, [ofres, sres], [ofres])
                    if kmajor:
                        dma("sy", dst_h[k, :, p * 512:(p + 1) * 512],
                            obf[:, :], rd=[ofres], wr=[rr(tag_h, k)])
                    else:
                        m0 = p * 4
                        dma("sy",
                            dst_h[m0:m0 + 4, :, k * 128:(k + 1) * 128]
                            .rearrange("m p c -> p m c"),
                            obf[:, :].rearrange("p (m c) -> p m c", c=128),
                            rd=[ofres],
                            wr=[rr(tag_h, mm_) for mm_ in range(m0, m0 + 4)])
                yield

        def post_wg():
            ts(mg127_t[:, :], mg_t[:, :], 1.0 / 127.0, None, ALU.mult,
               rd=[rr(mg_t)], wr=[rr(mg127_t)])

        def post_wd():
            tt(mud_t[:, :], mu_t[:, :], md_t[:, :], ALU.mult,
               rd=[rr(mu_t), rr(md_t)], wr=[rr(mud_t)])
            ts(mud_t[:, :], mud_t[:, :], 1.0 / (127.0 * 127.0), None, ALU.mult,
               rd=[rr(mud_t)], wr=[rr(mud_t)])

        prep_done = set()

        def prep_chain():
            for name, g in (
                    ("wOm", cast_gen(wOmT_h, sOm_h, NCH, NCH, sOm_h)),
                    ("w1", cast_gen(w1T_h, sW1_h, NH1, NCTX, sW1_h)),
                    ("wg", tern_gen(wgT_h, sWg_h, NCH, NIN, mg_t, sg_t,
                                    sWg_h, 0, False, post_wg)),
                    ("w2", cast_gen(w2T_h, sW2_h, NCH, NH1, sW2_h)),
                    ("wu", tern_gen(wuT_h, sWu_h, NCH, NIN, mu_t, su_t,
                                    sWu_h, 0, False)),
                    ("wd", tern_gen(wdT_h, sWd_h, NIN, NCH, md_t, sd_t,
                                    sWd_h, 0, True, post_wd))):
                for _ in g:
                    yield
                prep_done.add(name)

        prep_iter = prep_chain()

        def drain(nsteps):
            for _ in range(nsteps):
                try:
                    next(prep_iter)
                except StopIteration:
                    return

        def ensure(name):
            while name not in prep_done:
                try:
                    next(prep_iter)
                except StopIteration:
                    prep_done.add(name)
                    return

        ws_state = {"big": 0, "small": 0}

        def load_big(scr_h, m, c0, ncols):
            idx = ws_state["big"]
            ws_state["big"] = (idx + 1) % 3
            buf = wb[idx]
            apb = buf[:, :ncols]
            res = rr(buf)
            dma("sy", apb, scr_h[m, :, c0:c0 + ncols],
                rd=[rr(scr_h, m)], wr=[res])
            return apb, res

        def load_small(scr_h, m, ncols):
            idx = ws_state["small"]
            ws_state["small"] = (idx + 1) % 5
            buf = wsm[idx]
            apb = buf[:, :ncols]
            res = rr(buf)
            dma("sy", apb, scr_h[m, :, :ncols], rd=[rr(scr_h, m)], wr=[res])
            return apb, res

        def ln_finalize(D_, banks=(4, 5)):
            b0, b1 = banks
            act(mu_b[:, :], pb[b0][:, :], AF.Identity, scale=1.0 / D_,
                rd=[rr(pb[b0])], wr=[rr(mu_b)])
            act(aux_b[:, :], pb[b1][:, :], AF.Identity, scale=1.0 / D_,
                rd=[rr(pb[b1])], wr=[rr(aux_b)])
            act(aux2_b[:, :], mu_b[:, :], AF.Square,
                rd=[rr(mu_b)], wr=[rr(aux2_b)])
            tt(aux_b[:, :], aux_b[:, :], aux2_b[:, :], ALU.subtract,
               rd=[rr(aux_b), rr(aux2_b)], wr=[rr(aux_b)])
            act(aux2_b[:, :], aux_b[:, :], AF.Ln, bias=cst[:, 0:1],
                rd=[rr(aux_b), rr(cst)], wr=[rr(aux2_b)])
            act(rstd_b[:, :], aux2_b[:, :], AF.Exp, scale=-0.5,
                rd=[rr(aux2_b)], wr=[rr(rstd_b)])
            tt(nmr_b[:, :], mu_b[:, :], rstd_b[:, :], ALU.mult,
               rd=[rr(mu_b), rr(rstd_b)], wr=[rr(nmr_b)])

        def ln_stat_step(i, nck, a, res, stage="glb", banks=(4, 5)):
            b0, b1 = banks
            buf, bres = (sbA, rr(sbA)) if i % 2 == 0 else (sbB, rr(sbB))
            if stage == "glb":
                c2 = 14 + (i % 2)
                buf2, bres2 = glb[:, c2 * T:(c2 + 1) * T], rr(glb, c2)
            else:
                c2 = i % 2
                buf2, bres2 = hnb[:, c2 * T:(c2 + 1) * T], rr(hnb, c2)
            act(buf[:, :], a, AF.Identity, rd=[res], wr=[bres])
            mm(pb[b0][:, :], onebf[:, :], buf[:, :], i == 0, i == nck - 1,
               rd=[rr(onebf), bres], wr=[rr(pb[b0])])
            act(buf2, a, AF.Square, rd=[res], wr=[bres2])
            mm(pb[b1][:, :], onebf[:, :], buf2, i == 0, i == nck - 1,
               rd=[rr(onebf), bres2], wr=[rr(pb[b1])])

        def round_to_bf16(v_ap, v_res, out_ap, out_res):
            act(qi32[:, :], v_ap, AF.Identity, bias=cst[:, 2:3],
                rd=[v_res, rr(cst)], wr=[rr(qi32)])
            stt(csC[:, :], qi32[:, :], -1.0, v_ap, ALU.mult, ALU.add,
                rd=[rr(qi32), v_res], wr=[rr(csC)])
            ts(csD[:, :], csC[:, :], 0.5, -1.0, ALU.is_gt, ALU.add,
               rd=[rr(csC)], wr=[rr(csD)])
            stt(csD[:, :], qi32[:, :], 1.0, csD[:, :], ALU.mult, ALU.add,
                rd=[rr(qi32), rr(csD)], wr=[rr(csD)])
            stt(out_ap, csC[:, :], -0.5, csD[:, :], ALU.is_ge, ALU.add,
                rd=[rr(csC), rr(csD)], wr=[out_res])

        def bcast_pair(n_used, out0, out0_res, out1, out1_res):
            E.emit("te", lambda e:
                   e.transpose(pb[7][0:2 * NB, 0:128], tmc[:, :], identt[:, :]),
                   [rr(tmc), rr(identt)], [rr(pb[7])])
            vcp(rowst[:, :], pb[7][0:2 * NB, 0:128],
                rd=[rr(pb[7])], wr=[rr(rowst)])
            dma("gp", zrow[0:1, T:T + 2 * NB * 128], rowst[:, :],
                rd=[rr(rowst)], wr=[rr(zrow, 1)])
            E.emit("te", lambda e:
                   e.matmul(pb[7][:, :], onest[:, :], zrow[:, T:2 * T],
                            start=True, stop=True),
                   [rr(onest), rr(zrow, 1)], [rr(pb[7])])
            vcp(out0, pb[7][:, :], rd=[rr(pb[7])], wr=[out0_res])
            if n_used > 1:
                E.emit("te", lambda e:
                       e.matmul(pb[7][:, :], onest[:, :], zrow[:, 2 * T:3 * T],
                                start=True, stop=True),
                       [rr(onest), rr(zrow, 1)], [rr(pb[7])])
                vcp(out1, pb[7][:, :], rd=[rr(pb[7])], wr=[out1_res])

        def rowmax_acc(i, a, res):
            sq, sqres = ((ivp_b, rr(ivp_b)) if i % 2 == 0
                         else (cfin_b, rr(cfin_b)))
            tt(sq[:, :], a, a, ALU.mult, rd=[res], wr=[sqres])
            if i == 0:
                vcp(posc_b[:, :], sq[:, :], rd=[sqres], wr=[rr(posc_b)])
            else:
                tt(posc_b[:, :], posc_b[:, :], sq[:, :], ALU.max,
                   rd=[rr(posc_b), sqres], wr=[rr(posc_b)])

        def rowmax_finish(out_tm_ap, out_tm_res):
            for b in range(NB):
                E.emit("te", lambda e, b=b:
                       e.transpose(pb[6][:, b * 128:(b + 1) * 128],
                                   posc_b[:, b * 128:(b + 1) * 128],
                                   identt[:, :]),
                       [rr(posc_b), rr(identt)], [rr(pb[6])])
            for b in range(NB):
                E.emit("ve", lambda e, b=b:
                       e.reduce_max(tmm[:, b:b + 1],
                                    pb[6][:, b * 128:(b + 1) * 128], AX),
                       [rr(pb[6])], [rr(tmm)])
            ts(out_tm_ap, tmm[:, :], 1e-10, None, ALU.max,
               rd=[rr(tmm)], wr=[out_tm_res])

        def emit_tile(t):
            seq_t = t % TPS
            first_in_seq = (seq_t == 0)
            posc_ap = posc_b[:, :]
            ivp_ap = ivp_b[:, :]
            E.emit("ve", lambda e, off=float(seq_t * T):
                   e.tensor_tensor_scan(zrow[0:1, 0:T], prow[0:1, :],
                                        prow[0:1, :], off, ALU.add, ALU.bypass),
                   [rr(prow)], [rr(zrow, 0)], no_inc=True)
            vcp(csD[0:1, 0:T], zrow[0:1, 0:T],
                rd=[rr(zrow, 0)], wr=[rr(csD), ("possh", t)])
            E.emit("te", lambda e:
                   e.matmul(pb[6][:, :], onest[:, :], zrow[:, 0:T],
                            start=True, stop=True),
                   [rr(onest), rr(zrow, 0), ("possh", t)], [rr(pb[6])])
            act(aux_b[:, :], pb[6][:, :], AF.Ln, rd=[rr(pb[6])], wr=[rr(aux_b)])
            act(posc_b[:, :], aux_b[:, :], AF.Exp, scale=-0.5,
                rd=[rr(aux_b)], wr=[rr(posc_b)])
            act(ivp_b[:, :], aux_b[:, :], AF.Exp, scale=-1.0,
                rd=[rr(aux_b)], wr=[rr(ivp_b)])
            drain(4)
            xbf = hnb
            xq = glb

            for hh in range(2):
                dma("sy", x_tm[:].rearrange("p (c d) -> p c d", d=D),
                    x_h[t * T + hh * 256:t * T + (hh + 1) * 256, :]
                    .rearrange("(c p) d -> p c d", p=128),
                    rd=rrs(x_tm, range(2)), wr=rrs(x_tm, range(2)))
                for f in range(NCH):
                    for tc in range(2):
                        E.emit("te", lambda e, f=f, tc=tc:
                               e.transpose(pb[6][:, tc * 128:(tc + 1) * 128],
                                           x_tm[:, tc * D + f * 128:tc * D + (f + 1) * 128],
                                           identt[:, :]),
                               rrs(x_tm, range(2)) + [rr(identt)],
                               [rr(pb[6])])
                    h0 = hh * 256
                    act(s1[:, f * T + h0:f * T + h0 + 256], pb[6][:, 0:256],
                        AF.Identity, rd=[rr(pb[6])], wr=[rr(s1, f)])
                    act(xbf[:, f * T + h0:f * T + h0 + 256],
                        s1[:, f * T + h0:f * T + h0 + 256],
                        AF.Identity, rd=[rr(s1, f)], wr=[rr(hnb, f)])
                    drain(1)

            ensure("wOm")
            for m in range(NCH):
                slab, sres = load_small(sOm_h, m, D)
                bank = m % 2
                for k in range(NCH):
                    mm(pb[bank][:, :], slab[:, k * 128:(k + 1) * 128],
                       xbf[:, k * T:(k + 1) * T], k == 0, k == NCH - 1,
                       rd=[sres, rr(hnb, k)], wr=[rr(pb[bank])])
                act(csA[:, :], pb[bank][:, :], AF.Identity,
                    bias=bels_t[:, m:m + 1], scale=els_t[:, m:m + 1],
                    rd=[rr(pb[bank]), rr(els_t), rr(bels_t)], wr=[rr(csA)])
                if m >= 1:
                    for j, slot in enumerate((s2, s3, s4, s5)):
                        ln_stat_step(4 * (m - 1) + j, NCTX,
                                     slot[:, (m - 1) * T:m * T],
                                     rr(slot, m - 1))
                tt(csA[:, :], csA[:, :], posc_ap, ALU.mult,
                   rd=[rr(csA), rr(posc_b)], wr=[rr(csA)])
                scan(csB[:, :], csA[:, :],
                     0.0 if first_in_seq else phc[:, m:m + 1],
                     rd=[rr(csA)] + ([] if first_in_seq else [rr(phc, m)]),
                     wr=[rr(csB)])
                ts(qi32[:, :], csB[:, :], INV_2PI, 0.5, ALU.mult, ALU.add,
                   rd=[rr(csB)], wr=[rr(qi32)])
                vcp(csC[:, :], qi32[:, :], rd=[rr(qi32)], wr=[rr(csC)])
                stt(csD[:, :], csC[:, :], -CW1, csB[:, :], ALU.mult, ALU.add,
                    rd=[rr(csC), rr(csB)], wr=[rr(csD)])
                vcp(phc[:, m:m + 1], csB[:, T - 1:T],
                    rd=[rr(csB)], wr=[rr(phc, m)])
                stt(csD[:, :], csC[:, :], -CW2, csD[:, :], ALU.mult, ALU.add,
                    rd=[rr(csC), rr(csD)], wr=[rr(csD)])
                ts(csC[:, :], csD[:, :], -PI_B, None, ALU.is_lt,
                   rd=[rr(csD)], wr=[rr(csC)])
                stt(csB[:, :], csC[:, :], CW1, csD[:, :], ALU.mult, ALU.add,
                    rd=[rr(csC), rr(csD)], wr=[rr(csB)])
                ts(csC[:, :], csB[:, :], PIH_HI, None, ALU.is_gt,
                   rd=[rr(csB)], wr=[rr(csC)])
                stt(csD[:, :], csC[:, :], -CW1, csB[:, :], ALU.mult, ALU.add,
                    rd=[rr(csC), rr(csB)], wr=[rr(csD)])
                ts(csD[:, :], csD[:, :], PIH_LO, PIH_HI, ALU.add, ALU.add,
                   rd=[rr(csD)], wr=[rr(csD)])
                ts(csB[:, :], csB[:, :], -PI_B, PI_B, ALU.max, ALU.min,
                   rd=[rr(csB)], wr=[rr(csB)])
                ts(csD[:, :], csD[:, :], -PI_B, PI_B, ALU.max, ALU.min,
                   rd=[rr(csD)], wr=[rr(csD)])
                act(sinc[:, :], csB[:, :], AF.Sin, rd=[rr(csB)], wr=[rr(sinc)])
                act(cosc[:, :], csD[:, :], AF.Sin, rd=[rr(csD)], wr=[rr(cosc)])
                xc = s1[:, m * T:(m + 1) * T]
                gtt(s2[:, m * T:(m + 1) * T], xc, cosc[:, :], ALU.mult,
                    rd=[rr(s1, m), rr(cosc)], wr=[rr(s2, m)])
                gtt(s3[:, m * T:(m + 1) * T], xc, sinc[:, :], ALU.mult,
                    rd=[rr(s1, m), rr(sinc)], wr=[rr(s3, m)])
                scan(csA[:, :], s2[:, m * T:(m + 1) * T],
                     0.0 if first_in_seq else mrc[:, m:m + 1],
                     rd=[rr(s2, m)] + ([] if first_in_seq else [rr(mrc, m)]),
                     wr=[rr(csA)])
                scan(csB[:, :], s3[:, m * T:(m + 1) * T],
                     0.0 if first_in_seq else mic[:, m:m + 1],
                     rd=[rr(s3, m)] + ([] if first_in_seq else [rr(mic, m)]),
                     wr=[rr(csB)])
                tt(csC[:, :], csA[:, :], ivp_ap, ALU.mult,
                   rd=[rr(csA), rr(ivp_b)], wr=[rr(csC)])
                tt(csD[:, :], csB[:, :], ivp_ap, ALU.mult,
                   rd=[rr(csB), rr(ivp_b)], wr=[rr(csD)])
                vcp(mrc[:, m:m + 1], csA[:, T - 1:T],
                    rd=[rr(csA)], wr=[rr(mrc, m)])
                vcp(mic[:, m:m + 1], csB[:, T - 1:T],
                    rd=[rr(csB)], wr=[rr(mic, m)])
                tt(csA[:, :], csC[:, :], cosc[:, :], ALU.mult,
                   rd=[rr(csC), rr(cosc)], wr=[rr(csA)])
                tt(csB[:, :], csD[:, :], sinc[:, :], ALU.mult,
                   rd=[rr(csD), rr(sinc)], wr=[rr(csB)])
                tt(s4[:, m * T:(m + 1) * T], csA[:, :], csB[:, :], ALU.add,
                   rd=[rr(csA), rr(csB)], wr=[rr(s4, m)])
                tt(csA[:, :], csD[:, :], cosc[:, :], ALU.mult,
                   rd=[rr(csD), rr(cosc)], wr=[rr(csA)])
                tt(csB[:, :], csC[:, :], sinc[:, :], ALU.mult,
                   rd=[rr(csC), rr(sinc)], wr=[rr(csB)])
                tt(s5[:, m * T:(m + 1) * T], csA[:, :], csB[:, :], ALU.subtract,
                   rd=[rr(csA), rr(csB)], wr=[rr(s5, m)])
                drain(4)

            for nm, slot in (("cm_r", s2), ("cm_i", s3), ("ret_r", s4),
                             ("ret_i", s5)):
                if nm in dbg_hs:
                    dma("sy", dbg_hs[nm][t, :, :], slot[:, :],
                        rd=rrs(slot, range(NCH)))

            def ctx_ap(cc):
                slot = (s2, s3, s4, s5)[cc // 8]
                m = cc % 8
                return slot[:, m * T:(m + 1) * T], rr(slot, m)

            for j, slot in enumerate((s2, s3, s4, s5)):
                ln_stat_step(4 * 7 + j, NCTX, slot[:, 7 * T:8 * T],
                             rr(slot, 7))
            ln_finalize(float(CTX))
            for cc in range(NCTX):
                a, res = ctx_ap(cc)
                buf = csA if cc % 2 == 0 else csB
                gtt(buf[:, :], a, rstd_b[:, :], ALU.mult,
                    rd=[res, rr(rstd_b)], wr=[rr(buf)])
                tt(buf[:, :], buf[:, :], nmr_b[:, :], ALU.subtract,
                   rd=[rr(buf), rr(nmr_b)], wr=[rr(buf)])
                act(hnb[:, cc * T:(cc + 1) * T], buf[:, :], AF.Identity,
                    scale=lng_t[:, cc:cc + 1], bias=lnb_t[:, cc:cc + 1],
                    rd=[rr(buf), rr(lng_t), rr(lnb_t)], wr=[rr(hnb, cc)])
                drain(2)

            ensure("w1")
            for m in range(NH1):
                bank = m % 2
                for half in range(2):
                    slab, sres = load_big(sW1_h, m, half * 2048, 2048)
                    for kk in range(16):
                        k = half * 16 + kk
                        mm(pb[bank][:, :], slab[:, kk * 128:(kk + 1) * 128],
                           hnb[:, k * T:(k + 1) * T], k == 0, k == NCTX - 1,
                           rd=[sres, rr(hnb, k)], wr=[rr(pb[bank])])
                    drain(6)
                act(glb[:, m * T:(m + 1) * T], pb[bank][:, :], AF.Gelu,
                    bias=b1_t[:, m:m + 1],
                    rd=[rr(pb[bank]), rr(b1_t)], wr=[rr(glb, m)])

            ensure("w2")
            for m in range(NCH):
                slab, sres = load_big(sW2_h, m, 0, H1)
                bank = m % 2
                for k in range(NH1):
                    mm(pb[bank][:, :], slab[:, k * 128:(k + 1) * 128],
                       glb[:, k * T:(k + 1) * T], k == 0, k == NH1 - 1,
                       rd=[sres, rr(glb, k)], wr=[rr(pb[bank])])
                act(csA[:, :], pb[bank][:, :], AF.Identity, bias=b2_t[:, m:m + 1],
                    rd=[rr(pb[bank]), rr(b2_t)], wr=[rr(csA)])
                stt(s2[:, m * T:(m + 1) * T], s1[:, m * T:(m + 1) * T],
                    2.0, csA[:, :], ALU.mult, ALU.add,
                    rd=[rr(s1, m), rr(csA)], wr=[rr(s2, m)])
                ln_stat_step(m, NCH, s2[:, m * T:(m + 1) * T], rr(s2, m),
                             stage="hnb")
                drain(8)

            if "y1" in dbg_hs:
                dma("sy", dbg_hs["y1"][t, :, :], s2[:, :], rd=rrs(s2, range(NCH)))

            ln_finalize(float(D))
            for m in range(NCH):
                buf = csA if m % 2 == 0 else csB
                gtt(buf[:, :], s2[:, m * T:(m + 1) * T], rstd_b[:, :], ALU.mult,
                    rd=[rr(s2, m), rr(rstd_b)], wr=[rr(buf)])
                tt(s3[:, m * T:(m + 1) * T], buf[:, :], nmr_b[:, :],
                   ALU.subtract, rd=[rr(buf), rr(nmr_b)], wr=[rr(s3, m)])
                rowmax_acc(m, s3[:, m * T:(m + 1) * T], rr(s3, m))

            if "x2" in dbg_hs:
                dma("sy", dbg_hs["x2"][t, :, :], s3[:, :], rd=rrs(s3, range(NCH)))

            ensure("wd")
            rowmax_finish(tms[:, :], rr(tms))
            act(tmm[:, 0:NB], tms[:, :], AF.Ln, rd=[rr(tms)], wr=[rr(tmm)])
            act(tmc[:, 0:NB], tmm[:, 0:NB], AF.Exp, scale=-0.5, bias=cst[:, 1:2],
                rd=[rr(tmm), rr(cst)], wr=[rr(tmc)])
            act(tmm[:, 0:NB], tms[:, :], AF.Sqrt, rd=[rr(tms)], wr=[rr(tmm)])
            ts(tmc[:, NB:2 * NB], tmm[:, 0:NB], mg127_t[:, 0:1], None, ALU.mult,
               rd=[rr(tmm), rr(mg127_t)], wr=[rr(tmc)])
            bcast_pair(2, s1q_b[:, :], rr(s1q_b), cg_b[:, :], rr(cg_b))
            for m in range(NCH):
                gtt(csB[:, :], s3[:, m * T:(m + 1) * T], s1q_b[:, :],
                    ALU.mult, rd=[rr(s3, m), rr(s1q_b)], wr=[rr(csB)])
                round_to_bf16(csB[:, :], rr(csB),
                              xq[:, m * T:(m + 1) * T], rr(glb, m))
                drain(2)

            def hf_ap(o):
                slot = (s1, s2, s4, s5)[o // 8]
                m = o % 8
                return slot[:, m * T:(m + 1) * T], rr(slot, m)

            ensure("wu")
            for o in range(NIN):
                gslab, gres = load_small(sWg_h, o, D)
                uslab, ures = load_small(sWu_h, o, D)
                gbank, ubank = (0, 1) if o % 2 == 0 else (2, 3)
                gst = gps0 if o % 2 == 0 else gps1
                ust = ups0 if o % 2 == 0 else ups1
                for k in range(NCH):
                    mm(pb[gbank][:, :], gslab[:, k * 128:(k + 1) * 128],
                       xq[:, k * T:(k + 1) * T], k == 0, k == NCH - 1,
                       rd=[gres, rr(glb, k)], wr=[rr(pb[gbank])])
                for k in range(NCH):
                    mm(pb[ubank][:, :], uslab[:, k * 128:(k + 1) * 128],
                       xq[:, k * T:(k + 1) * T], k == 0, k == NCH - 1,
                       rd=[ures, rr(glb, k)], wr=[rr(pb[ubank])])
                act(gst[:, :], pb[gbank][:, :], AF.Identity,
                    rd=[rr(pb[gbank])], wr=[rr(gst)])
                act(ust[:, :], pb[ubank][:, :], AF.Identity,
                    rd=[rr(pb[ubank])], wr=[rr(ust)])
                tt(csA[:, :], gst[:, :], cg_b[:, :], ALU.mult,
                   rd=[rr(gst), rr(cg_b)], wr=[rr(csA)])
                act(csB[:, :], csA[:, :], AF.Silu, rd=[rr(csA)], wr=[rr(csB)])
                ha, hres = hf_ap(o)
                E.emit("gp", lambda e, ha=ha, u=ust:
                       e.tensor_tensor(ha, csB[:, :], u[:, :], ALU.mult),
                       [rr(csB), rr(ust)], [hres])
                rowmax_acc(o, ha, hres)
                drain(4)

            rowmax_finish(tml[:, 0:NB], rr(tml))
            act(tmm[:, 0:NB], tml[:, 0:NB], AF.Ln, rd=[rr(tml)], wr=[rr(tmm)])
            act(tmc[:, 0:NB], tmm[:, 0:NB], AF.Exp, scale=-0.5, bias=cst[:, 1:2],
                rd=[rr(tmm), rr(cst)], wr=[rr(tmc)])
            act(tmq[:, 0:NB], tml[:, 0:NB], AF.Sqrt, rd=[rr(tml)], wr=[rr(tmq)])
            act(tmm[:, 0:NB], tms[:, :], AF.Sqrt, rd=[rr(tms)], wr=[rr(tmm)])
            tt(tmc[:, NB:2 * NB], tmq[:, 0:NB], tmm[:, 0:NB], ALU.mult,
               rd=[rr(tmq), rr(tmm)], wr=[rr(tmc)])
            ts(tmc[:, NB:2 * NB], tmc[:, NB:2 * NB], mud_t[:, 0:1], None, ALU.mult,
               rd=[rr(tmc), rr(mud_t)], wr=[rr(tmc)])
            bcast_pair(2, s2q_b[:, :], rr(s2q_b), cfin_b[:, :], rr(cfin_b))
            for o in range(NIN):
                ha, hres = hf_ap(o)
                gtt(csB[:, :], ha, s2q_b[:, :], ALU.mult,
                    rd=[hres, rr(s2q_b)], wr=[rr(csB)])
                round_to_bf16(csB[:, :], rr(csB),
                              hnb[:, o * T:(o + 1) * T], rr(hnb, o))
                drain(1)

            if "scales" in dbg_hs:
                for i, (bt, res) in enumerate([(s1q_b, rr(s1q_b)), (cg_b, rr(cg_b)),
                                               (s2q_b, rr(s2q_b)), (cfin_b, rr(cfin_b))]):
                    dma("sy", dbg_hs["scales"][t, :, i * T:(i + 1) * T], bt[:, :],
                        rd=[res])
            if "hf0" in dbg_hs:
                for i, slot in enumerate((s1, s2, s4, s5)):
                    dma("sy", dbg_hs["hf%d" % i][t, :, :], slot[:, :],
                        rd=rrs(slot, range(NCH)))

            for k in range(NIN):
                slab, sres = load_small(sWd_h, k, D)
                for m in range(NCH):
                    mm(pb[m][:, :], slab[:, m * 128:(m + 1) * 128],
                       hnb[:, k * T:(k + 1) * T], k == 0, k == NIN - 1,
                       rd=[sres, rr(hnb, k)], wr=[rr(pb[m])])
            evord = [6, 7, 0, 1, 2, 3, 4, 5]
            for p, m in enumerate(evord):
                st = gps0 if p % 2 == 0 else gps1
                act(st[:, :], pb[m][:, :], AF.Identity,
                    rd=[rr(pb[m])], wr=[rr(st)])
                gtt(csA[:, :], st[:, :], cfin_b[:, :], ALU.mult,
                    rd=[rr(st), rr(cfin_b)], wr=[rr(csA)])
                tt(s4[:, m * T:(m + 1) * T], csA[:, :],
                   s3[:, m * T:(m + 1) * T], ALU.add,
                   rd=[rr(csA), rr(s3, m)], wr=[rr(s4, m)])
                if p >= 2:
                    ms = evord[p - 2]
                    ln_stat_step(p - 2, NCH, s4[:, ms * T:(ms + 1) * T],
                                 rr(s4, ms), stage="hnb", banks=(6, 7))
                drain(2)
            for p in (6, 7):
                ms = evord[p]
                ln_stat_step(p, NCH, s4[:, ms * T:(ms + 1) * T],
                             rr(s4, ms), stage="hnb", banks=(6, 7))

            if "y2" in dbg_hs:
                dma("sy", dbg_hs["y2"][t, :, :], s4[:, :], rd=rrs(s4, range(NCH)))

            ln_finalize(float(D), banks=(6, 7))
            for m in range(NCH):
                buf = csA if m % 2 == 0 else csB
                gtt(buf[:, :], s4[:, m * T:(m + 1) * T], rstd_b[:, :], ALU.mult,
                    rd=[rr(s4, m), rr(rstd_b)], wr=[rr(buf)])
                tt(s5[:, m * T:(m + 1) * T], buf[:, :], nmr_b[:, :],
                   ALU.subtract, rd=[rr(buf), rr(nmr_b)], wr=[rr(s5, m)])
                drain(2)

            for hh in range(2):
                for tc in range(2):
                    for f in range(NCH):
                        bank = 6 if f < 4 else 7
                        E.emit("te", lambda e, tc=tc, f=f, bank=bank, hh=hh:
                               e.transpose(pb[bank][:, (f % 4) * 128:(f % 4 + 1) * 128],
                                           s5[:, f * T + (hh * 2 + tc) * 128:
                                              f * T + (hh * 2 + tc + 1) * 128],
                                           identt[:, :]),
                               [rr(s5, f), rr(identt)], [rr(pb[bank])])
                    act(x_tm[:, tc * D:tc * D + 512], pb[6][:, :],
                        AF.Identity, rd=[rr(pb[6])], wr=[rr(x_tm, tc)])
                    act(x_tm[:, tc * D + 512:(tc + 1) * D], pb[7][:, :],
                        AF.Identity, rd=[rr(pb[7])], wr=[rr(x_tm, tc)])
                dma("sy",
                    out_h[t * T + hh * 256:t * T + (hh + 1) * 256, :]
                    .rearrange("(c p) d -> p c d", p=128),
                    x_tm[:].rearrange("p (c d) -> p c d", d=D),
                    rd=rrs(x_tm, range(2)), wr=[("out", t, hh)])

        for t in range(NT):
            emit_tile(t)

        E.build(block, sems, dma_sems)

    return nc


_prog_cache = {}


def _get_prog():
    key = tuple(_DBG_WANT)
    if key not in _prog_cache:
        _prog_cache[key] = build_program()
    return _prog_cache[key]


def _pp(vec, nch):
    return np.ascontiguousarray(np.asarray(vec, np.float32).reshape(nch, 128).T)


def kernel(L0_attn_x, L0_attn_W_omega, L0_attn_b_omega, L0_attn_log_scale,
           L0_attn_ln_gamma, L0_attn_ln_beta, L0_attn_W1, L0_attn_b1,
           L0_attn_W2, L0_attn_b2, L0_ffn_w_gate, L0_ffn_w_up, L0_ffn_w_down):
    x = np.asarray(L0_attn_x, np.float32)
    shared = dict(
        wOmT=np.ascontiguousarray(np.asarray(L0_attn_W_omega, np.float32).T),
        w1T=np.ascontiguousarray(np.asarray(L0_attn_W1, np.float32).T),
        w2T=np.ascontiguousarray(np.asarray(L0_attn_W2, np.float32).T),
        wgT=np.ascontiguousarray(np.asarray(L0_ffn_w_gate, np.float32).T),
        wuT=np.ascontiguousarray(np.asarray(L0_ffn_w_up, np.float32).T),
        wdT=np.ascontiguousarray(np.asarray(L0_ffn_w_down, np.float32).T),
        bom=_pp(L0_attn_b_omega, NCH),
        lsc=_pp(L0_attn_log_scale, NCH),
        lng=_pp(L0_attn_ln_gamma, NCTX),
        lnb=_pp(L0_attn_ln_beta, NCTX),
        b1v=_pp(L0_attn_b1, NH1),
        b2v=_pp(L0_attn_b2, NCH),
        ident=np.eye(128, dtype=np.float32),
        ones32=np.ones((128, 128), np.float32),
    )
    in_maps = []
    for c in range(NCORE):
        m = dict(shared)
        m["x"] = np.ascontiguousarray(
            x[c * SEQ_PER_CORE:(c + 1) * SEQ_PER_CORE].reshape(TOK, D))
        in_maps.append(m)
    nc = _get_prog()
    res = run_bass_kernel_spmd(nc, in_maps, core_ids=list(range(NCORE)))
    out = np.empty((B, S, D), np.float32)
    for c in range(NCORE):
        out[c * SEQ_PER_CORE:(c + 1) * SEQ_PER_CORE] = \
            np.asarray(res.results[c]["out"], np.float32).reshape(
                SEQ_PER_CORE, S, D)
    return out
